# version 08
# baseline (speedup 1.0000x reference)
from contextlib import ExitStack

import numpy as np
import ml_dtypes
import concourse.bass as bass
import concourse.mybir as mybir
from concourse.bass_utils import run_bass_kernel_spmd

F32 = mybir.dt.float32
BF16 = mybir.dt.bfloat16
AF = mybir.ActivationFunctionType
ALU = mybir.AluOpType
AX = mybir.AxisListType
NBF = ml_dtypes.bfloat16

B, L, D = 2, 16384, 1024
HID, NH, HDIM, BS = 3072, 16, 64, 256
NCORES = 8
TPC = (B * L) // NCORES
TILE = 1024
NT = TPC // TILE
DCH = D // 128
HCH = HID // 128
SD = 0.5 / (127.0 * 127.0)
NBH = 4 * NH

_SUBLAYERS = 3

FFN_SLOTS = 72
ATTN_SLOTS = 32
TILE_SLOTS = FFN_SLOTS + ATTN_SLOTS + FFN_SLOTS


class Op:
    __slots__ = ("eng", "fn", "dma", "deps", "needs_inc", "cum", "semkey",
                 "hazard")

    def __init__(self, eng, fn, dma):
        self.eng, self.fn, self.dma = eng, fn, dma
        self.deps = []
        self.needs_inc = False
        self.cum = None
        self.hazard = False
        self.semkey = "D" if (eng == "G" and dma) else eng


class Sched:

    def __init__(self):
        self.ops = {e: [] for e in "SPVAG"}
        self.bufs = {}

    def _buf(self, name):
        b = self.bufs.get(name)
        if b is None:
            b = {"w": None, "r": []}
            self.bufs[name] = b
        return b

    def op(self, eng, fn, reads=(), writes=(), dma=False, semkey=None,
           hazard=False):
        o = Op(eng, fn, dma)
        o.hazard = hazard
        if semkey is not None:
            o.semkey = semkey
        deps = set()
        for n in reads:
            b = self._buf(n)
            if b["w"] is not None:
                deps.add(b["w"])
        for n in writes:
            b = self._buf(n)
            if b["w"] is not None:
                deps.add(b["w"])
            deps.update(b["r"])
        for n in reads:
            self._buf(n)["r"].append(o)
        for n in writes:
            b = self._buf(n)
            b["w"] = o
            b["r"] = []
        for d in deps:
            if d is o:
                continue
            if d.eng == eng and not d.dma and not d.hazard:
                continue
            o.deps.append(d)
            d.needs_inc = True
        self.ops[eng].append(o)
        return o

    SEMKEYS = tuple("SPVAGD") + ("K0", "K1", "K2", "C0", "C1", "X", "X2",
                                 "R0", "R1", "R2", "R3",
                                 "T0", "T1", "T2", "T3")

    def emit(self, block, sems):
        totals = {k: 0 for k in self.SEMKEYS}
        for e, lst in self.ops.items():
            for o in lst:
                if o.needs_inc or o.dma:
                    o.needs_inc = True
                    totals[o.semkey] += 16 if o.dma else 1
                o.cum = totals[o.semkey]

        def body_for(e):
            def body(eng):
                hw = {k: 0 for k in totals}
                for o in self.ops[e]:
                    need = {}
                    for d in o.deps:
                        if d.cum > need.get(d.semkey, 0):
                            need[d.semkey] = d.cum
                    for pe, v in sorted(need.items()):
                        if v > hw[pe]:
                            eng.wait_ge(sems[pe], v)
                            hw[pe] = v
                    inst = o.fn(eng)
                    if o.needs_inc:
                        inst.then_inc(sems[o.semkey], 16 if o.dma else 1)
                if e == "S":
                    for k in ("S", "K0", "K1", "K2", "C0", "C1", "X", "X2",
                              "R0", "R1", "R2", "R3"):
                        if totals[k] > 0:
                            eng.wait_ge(sems[k], totals[k])
                if e == "A":
                    for k in ("T0", "T1", "T2", "T3"):
                        if totals[k] > 0:
                            eng.wait_ge(sems[k], totals[k])
                if e == "G" and totals["D"] > 0:
                    eng.wait_ge(sems["D"], totals["D"])
            return body

        block.sync(body_for("S"))
        block.tensor(body_for("P"))
        block.vector(body_for("V"))
        block.scalar(body_for("A"))
        block.gpsimd(body_for("G"))


_DEBUG_TAPS = False
_DEBUG_ATTN = True


def build_program(sublayers=3, debug_taps=False, debug_attn=True):
    nc = bass.Bass("TRN2", target_bir_lowering=False)

    xin = nc.dram_tensor("xin", [D, TPC], F32, kind="ExternalInput").ap()
    if debug_taps:
        dbg_h = nc.dram_tensor("dbg_h", [HCH, 128, 1024], BF16, kind="ExternalOutput").ap()
        dbg_ao = nc.dram_tensor("dbg_ao", [DCH, 128, 1024], BF16, kind="ExternalOutput").ap()
        dbg_xn = nc.dram_tensor("dbg_xn", [DCH, 128, 1024], BF16, kind="ExternalOutput").ap()
        dbg_x2 = nc.dram_tensor("dbg_x2", [DCH, 128, 1024], F32, kind="ExternalOutput").ap()
        dbg_xn3 = nc.dram_tensor("dbg_xn3", [DCH, 128, 1024], BF16, kind="ExternalOutput").ap()
        dbg_h3 = nc.dram_tensor("dbg_h3", [HCH, 128, 1024], BF16, kind="ExternalOutput").ap()
    xout = nc.dram_tensor("xout", [D, TPC], F32, kind="ExternalOutput").ap()
    ffn_w = nc.dram_tensor("ffn_w", [FFN_SLOTS, 128, 1024], BF16, kind="ExternalInput").ap()
    attn_w = nc.dram_tensor("attn_w", [ATTN_SLOTS, 128, 1024], F32, kind="ExternalInput").ap()
    cos_t = nc.dram_tensor("cos_t", [128, TPC], F32, kind="ExternalInput").ap()
    sin_t = nc.dram_tensor("sin_t", [128, TPC], F32, kind="ExternalInput").ap()
    mask_t = nc.dram_tensor("mask_t", [128, 512], F32, kind="ExternalInput").ap()
    idn_t = nc.dram_tensor("idn_t", [128, 128], BF16, kind="ExternalInput").ap()
    perm_t = nc.dram_tensor("perm_t", [128, 128], BF16, kind="ExternalInput").ap()
    attn_wb = nc.dram_tensor("attn_wb", [ATTN_SLOTS, 128, 1024], BF16, kind="Internal").ap()

    with ExitStack() as ctx:
        def sb(name, shape, dt):
            return ctx.enter_context(nc.sbuf_tensor(name, shape, dt))

        x = [sb(f"x{c}", [128, 1024], F32) for c in range(DCH)]
        xn = [sb(f"xn{c}", [128, 1024], BF16) for c in range(DCH)]
        h = [sb(f"h{c}", [128, 1024], BF16) for c in range(HCH)]
        ao = [sb(f"ao{c}", [128, 1024], BF16) for c in range(DCH)]
        ring = sb("ring", [128, 8 * 1024], BF16)
        cosb = sb("cosb", [128, 1024], F32)
        sinb = sb("sinb", [128, 1024], F32)
        ts_ = [sb(f"t{i}", [128, 1024], F32) for i in range(4)]
        bs = [sb(f"bs{i}", [128, 1024], BF16) for i in range(2)]
        sgb = [sb(f"sgb{i}", [128, 512], BF16) for i in range(2)]
        smv = [sb(f"smv{i}", [128, 512], F32) for i in range(4)]
        ae = [sb(f"ae{i}", [128, 512], BF16) for i in range(4)]
        amn = [sb(f"amn{i}", [128, 512], BF16) for i in range(4)]
        atT = [sb(f"atT{i}", [128, 512], BF16) for i in range(4)]
        negm = [sb(f"negm{i}", [128, 2], F32) for i in range(4)]
        den4 = [sb(f"den4{i}", [128, 4], F32) for i in range(4)]
        rden4 = [sb(f"rden4{i}", [128, 4], F32) for i in range(4)]
        rowsA = sb("rowsA", [1, 1024], F32)
        rowsB = sb("rowsB", [1, 1024], F32)
        rowsC = sb("rowsC", [1, 1024], F32)
        maskb = sb("maskb", [128, 512], F32)
        identb = sb("identb", [128, 128], BF16)
        permb = sb("permb", [128, 128], BF16)
        onesc = sb("onesc", [128, 1], BF16)
        onesr = sb("onesr", [1, 128], F32)
        epsb = sb("epsb", [128, 1], F32)
        pb = [ctx.enter_context(nc.psum_tensor(f"pb{i}", [128, 512], F32)) for i in range(8)]

        sems = {e: ctx.enter_context(nc.semaphore(f"sem{e}")) for e in Sched.SEMKEYS}
        block = ctx.enter_context(nc.Block())

        S = Sched()

        def rname(i):
            return f"ring{i % 8}"

        def rslot(i):
            j = i % 8
            return ring[:, j * 1024:(j + 1) * 1024]

        S.op("S", lambda e: e.dma_start(out=maskb[:, :], in_=mask_t), writes=["maskb"],
             dma=True, semkey="K0")
        S.op("S", lambda e: e.dma_start(out=identb[:, :], in_=idn_t), writes=["identb"],
             dma=True, semkey="K1")
        S.op("S", lambda e: e.dma_start(out=permb[:, :], in_=perm_t), writes=["permb"],
             dma=True, semkey="K2")
        S.op("V", lambda e: e.memset(onesc[:, :], 1.0), writes=["onesc"])
        S.op("V", lambda e: e.memset(onesr[:, :], 1.0), writes=["onesr"])
        S.op("V", lambda e: e.memset(epsb[:, :], 1e-5), writes=["epsb"])
        for i in range(4):
            S.op("V", lambda e, i=i: e.memset(den4[i][:, :], 1.0), writes=[f"den4{i}"])
        for j in range(0, ATTN_SLOTS, 4):
            S.op("G", lambda e, j=j: e.dma_start(out=attn_wb[j:j + 4], in_=attn_w[j:j + 4]),
                 writes=[f"asb{k}" for k in range(j, j + 4)], dma=True)
        S.op("G", lambda e: e.engine_nop(),
             reads=[f"asb{k}" for k in range(ATTN_SLOTS)], writes=["asball"])

        state = {"loaded": 0}

        def src_of(slot):
            s = slot % TILE_SLOTS
            if s < FFN_SLOTS:
                return ffn_w, s, []
            if s < FFN_SLOTS + ATTN_SLOTS:
                j = s - FFN_SLOTS
                return attn_wb, j, ["asball"]
            return ffn_w, s - FFN_SLOTS - ATTN_SLOTS, []

        def need_slot(gi):
            while state["loaded"] <= gi:
                i = state["loaded"]
                arr, j, rd = src_of(i)
                arr2, j2, rd2 = src_of(i + 1)
                assert arr is arr2 and j2 == j + 1 and i % 2 == 0
                pos = i % 8
                out_ap = ring[:, pos * 1024:(pos + 2) * 1024].rearrange(
                    "p (s j) -> p s j", s=2)
                in_ap = arr[j:j + 2].rearrange("s p j -> p s j")
                S.op("S", lambda e, o=out_ap, i_=in_ap: e.dma_start(out=o, in_=i_),
                     reads=rd + rd2, writes=[rname(i), rname(i + 1)], dma=True,
                     semkey=f"R{(i % 8) // 2}")
                state["loaded"] += 2

        def emit_ln(t):
            for c in range(DCH):
                S.op("A", lambda e, c=c: e.copy(out=xn[c][:, :], in_=x[c][:, :]),
                     reads=[f"x{c}"], writes=[f"xn{c}"])
                p = c % 2
                S.op("A", lambda e, c=c, p=p: e.activation(
                    out=bs[p][:, :], in_=xn[c][:, :], func=AF.Square),
                    reads=[f"xn{c}"], writes=[f"bs{p}"])
                for hf in range(2):
                    S.op("P", lambda e, c=c, hf=hf: e.matmul(
                        pb[4 + hf][0:1, 0:512], onesc[:, :],
                        xn[c][:, hf * 512:(hf + 1) * 512],
                        start=(c == 0), stop=(c == DCH - 1)),
                        reads=[f"xn{c}", "onesc"], writes=[f"pb{4 + hf}"])
                    S.op("P", lambda e, c=c, p=p, hf=hf: e.matmul(
                        pb[6 + hf][0:1, 0:512], onesc[:, :],
                        bs[p][:, hf * 512:(hf + 1) * 512],
                        start=(c == 0), stop=(c == DCH - 1)),
                        reads=[f"bs{p}", "onesc"], writes=[f"pb{6 + hf}"])
            for hf in range(2):
                S.op("V", lambda e, hf=hf: e.tensor_scalar(
                    out=rowsA[0:1, hf * 512:(hf + 1) * 512], in0=pb[4 + hf][0:1, 0:512],
                    scalar1=-1.0 / D, scalar2=None, op0=ALU.mult),
                    reads=[f"pb{4 + hf}"], writes=["rowsA"])
            for hf in range(2):
                S.op("P", lambda e, hf=hf: e.matmul(
                    pb[2 + hf][:, 0:512], onesr[0:1, :], rowsA[0:1, hf * 512:(hf + 1) * 512],
                    start=True, stop=True),
                    reads=["rowsA", "onesr"], writes=[f"pb{2 + hf}"])
                S.op("A", lambda e, hf=hf: e.copy(
                    out=ts_[1][:, hf * 512:(hf + 1) * 512], in_=pb[2 + hf][:, 0:512]),
                    reads=[f"pb{2 + hf}"], writes=["t1"])
            for hf in range(2):
                S.op("A", lambda e, hf=hf: e.activation(
                    out=rowsC[0:1, hf * 512:(hf + 1) * 512], in_=pb[4 + hf][0:1, 0:512],
                    func=AF.Square),
                    reads=[f"pb{4 + hf}"], writes=["rowsC"])
                S.op("V", lambda e, hf=hf: e.scalar_tensor_tensor(
                    out=rowsB[0:1, hf * 512:(hf + 1) * 512],
                    in0=rowsC[0:1, hf * 512:(hf + 1) * 512], scalar=-1.0 / D,
                    in1=pb[6 + hf][0:1, 0:512], op0=ALU.mult, op1=ALU.add),
                    reads=["rowsC", f"pb{6 + hf}"], writes=["rowsB"])
            S.op("A", lambda e: e.activation(
                out=rowsB[0:1, 0:1024], in_=rowsB[0:1, 0:1024], func=AF.Ln,
                scale=1.0 / D, bias=epsb[0:1, 0:1]),
                reads=["rowsB", "epsb"], writes=["rowsB"])
            S.op("A", lambda e: e.activation(
                out=rowsC[0:1, 0:1024], in_=rowsB[0:1, 0:1024], func=AF.Exp,
                scale=-0.5),
                reads=["rowsB"], writes=["rowsC"])
            for hf in range(2):
                S.op("P", lambda e, hf=hf: e.matmul(
                    pb[hf][:, 0:512], onesr[0:1, :], rowsC[0:1, hf * 512:(hf + 1) * 512],
                    start=True, stop=True),
                    reads=["rowsC", "onesr"], writes=[f"pb{hf}"])
                S.op("A", lambda e, hf=hf: e.copy(
                    out=ts_[0][:, hf * 512:(hf + 1) * 512], in_=pb[hf][:, 0:512]),
                    reads=[f"pb{hf}"], writes=["t0"])
            for c in range(DCH):
                tp = 2 + (c % 2)
                S.op("G", lambda e, c=c, tp=tp: e.tensor_add(
                    ts_[tp][:, :], x[c][:, :], ts_[1][:, :]),
                    reads=[f"x{c}", "t1"], writes=[f"t{tp}"])
                S.op("V", lambda e, c=c, tp=tp: e.tensor_mul(
                    xn[c][:, :], ts_[tp][:, :], ts_[0][:, :]),
                    reads=[f"t{tp}", "t0"], writes=[f"xn{c}"])

        def emit_ffn(t, base_slot, store_x=False):
            def gu_F0(p):
                hc, th = divmod(p, 2)
                gslot, uslot = base_slot + 2 * hc, base_slot + 2 * hc + 1
                need_slot(uslot)
                gb_, ub_ = (0, 4) if hc % 2 == 0 else (2, 6)
                for kc in range(DCH):
                    S.op("P", lambda e, gslot=gslot, gb_=gb_, th=th, kc=kc: e.matmul(
                        pb[gb_ + th][:, 0:512], rslot(gslot)[:, kc * 128:(kc + 1) * 128],
                        xn[kc][:, th * 512:(th + 1) * 512],
                        start=(kc == 0), stop=(kc == DCH - 1)),
                        reads=[rname(gslot), f"xn{kc}"], writes=[f"pb{gb_ + th}"])
                for kc in range(DCH):
                    S.op("P", lambda e, uslot=uslot, ub_=ub_, th=th, kc=kc: e.matmul(
                        pb[ub_ + th][:, 0:512], rslot(uslot)[:, kc * 128:(kc + 1) * 128],
                        xn[kc][:, th * 512:(th + 1) * 512],
                        start=(kc == 0), stop=(kc == DCH - 1)),
                        reads=[rname(uslot), f"xn{kc}"], writes=[f"pb{ub_ + th}"])

            def gu_F1(p):
                hc, th = divmod(p, 2)
                gb_ = 0 if hc % 2 == 0 else 2
                S.op("A", lambda e, gb_=gb_, th=th: e.activation(
                    out=sgb[th][:, :], in_=pb[gb_ + th][:, 0:512], func=AF.Silu,
                    scale=1.0 / 127.0),
                    reads=[f"pb{gb_ + th}"], writes=[f"sgb{th}"])

            def gu_F2(p):
                hc, th = divmod(p, 2)
                ub_ = 4 if hc % 2 == 0 else 6
                S.op("V", lambda e, hc=hc, ub_=ub_, th=th: e.tensor_mul(
                    h[hc][:, th * 512:(th + 1) * 512], sgb[th][:, :],
                    pb[ub_ + th][:, 0:512]),
                    reads=[f"sgb{th}", f"pb{ub_ + th}"], writes=[f"h{hc}"])

            gu_stages = [gu_F0, gu_F1, gu_F2]
            for tt in range(2 * HCH + 2):
                for k, st in enumerate(gu_stages):
                    p = tt - k
                    if 0 <= p < 2 * HCH:
                        st(p)
            for pas in range(2):
                for hc in range(HCH):
                    slot = base_slot + 48 + pas * 12 + hc // 2
                    need_slot(slot)
                    for oc in range(4):
                        off = (((hc % 2) * 4) + oc) * 128
                        for th in range(2):
                            S.op("P", lambda e, slot=slot, off=off, oc=oc, th=th, hc=hc: e.matmul(
                                pb[oc * 2 + th][:, 0:512],
                                rslot(slot)[:, off:off + 128],
                                h[hc][:, th * 512:(th + 1) * 512],
                                start=(hc == 0), stop=(hc == HCH - 1)),
                                reads=[rname(slot), f"h{hc}"], writes=[f"pb{oc * 2 + th}"])
                for oc in range(4):
                    xc = pas * 4 + oc
                    for th in range(2):
                        S.op("V", lambda e, oc=oc, th=th, xc=xc: e.scalar_tensor_tensor(
                            out=x[xc][:, th * 512:(th + 1) * 512],
                            in0=pb[oc * 2 + th][:, 0:512], scalar=SD,
                            in1=x[xc][:, th * 512:(th + 1) * 512],
                            op0=ALU.mult, op1=ALU.add),
                            reads=[f"pb{oc * 2 + th}", f"x{xc}"], writes=[f"x{xc}"])
                if store_x:
                    for oc in range(4):
                        xc = pas * 4 + oc
                        S.op("G", lambda e, xc=xc, t=t: e.dma_start(
                            out=xout[xc * 128:(xc + 1) * 128, t * 1024:(t + 1) * 1024],
                            in_=x[xc][:, :]),
                            reads=[f"x{xc}"], writes=[f"xo{xc}"], dma=True)
                    S.op("G", lambda e: e.engine_nop(),
                         reads=[f"xo{pas * 4 + oc}" for oc in range(4)],
                         writes=[f"xstb{pas}"])

        def emit_attn(t, base_slot):
            tc0 = t * 1024
            S.op("S", lambda e, t=t: e.dma_start(
                out=cosb[:, :], in_=cos_t[:, t * 1024:(t + 1) * 1024]),
                writes=["cosb"], dma=True, semkey="C0")
            S.op("S", lambda e, t=t: e.dma_start(
                out=sinb[:, :], in_=sin_t[:, t * 1024:(t + 1) * 1024]),
                writes=["sinb"], dma=True, semkey="C1")
            def qk_Q0(j):
                qk, oc = divmod(j, DCH)
                slot = base_slot + qk * 8 + oc
                need_slot(slot)
                qb_ = (0, 1) if j % 2 == 0 else (2, 3)
                for th in range(2):
                    for kc in range(DCH):
                        S.op("P", lambda e, slot=slot, qb_=qb_, th=th, kc=kc: e.matmul(
                            pb[qb_[th]][:, 0:512], rslot(slot)[:, kc * 128:(kc + 1) * 128],
                            xn[kc][:, th * 512:(th + 1) * 512],
                            start=(kc == 0), stop=(kc == DCH - 1)),
                            reads=[rname(slot), f"xn{kc}"], writes=[f"pb{qb_[th]}"])

            def qk_Q1(j):
                qk = j // DCH
                par = j % 2
                qb_ = (0, 1) if par == 0 else (2, 3)
                sw_ = (4, 5) if par == 0 else (6, 7)
                for th in range(2):
                    S.op("A", lambda e, qb_=qb_, th=th, par=par, qk=qk: e.activation(
                        out=bs[par][:, th * 512:(th + 1) * 512], in_=pb[qb_[th]][:, 0:512],
                        func=AF.Copy, scale=(0.125 if qk == 0 else 1.0)),
                        reads=[f"pb{qb_[th]}"], writes=[f"bs{par}"])
                    S.op("P", lambda e, sw_=sw_, th=th, par=par: e.matmul(
                        pb[sw_[th]][:, 0:512], permb[:, :],
                        bs[par][:, th * 512:(th + 1) * 512], start=True, stop=True),
                        reads=["permb", f"bs{par}"], writes=[f"pb{sw_[th]}"])

            def qk_Q2(j):
                par = j % 2
                sw_ = (4, 5) if par == 0 else (6, 7)
                tA, tB = 2 + par, 0 + par
                S.op("V", lambda e, par=par, tA=tA: e.tensor_mul(
                    ts_[tA][:, :], bs[par][:, :], cosb[:, 0:1024]),
                    reads=[f"bs{par}", "cosb"], writes=[f"t{tA}"])
                for th in range(2):
                    S.op("V", lambda e, sw_=sw_, th=th, tB=tB: e.tensor_mul(
                        ts_[tB][:, th * 512:(th + 1) * 512], pb[sw_[th]][:, 0:512],
                        sinb[:, th * 512:(th + 1) * 512]),
                        reads=[f"pb{sw_[th]}", "sinb"], writes=[f"t{tB}"])

            def qk_Q3(j):
                par = j % 2
                tA, tB = 2 + par, 0 + par
                S.op("G", lambda e, hdst=j, tA=tA, tB=tB: e.tensor_add(
                    h[hdst][:, :], ts_[tA][:, :], ts_[tB][:, :]),
                    reads=[f"t{tA}", f"t{tB}"], writes=[f"h{j}"])

            qk_stages = [qk_Q0, qk_Q1, qk_Q2, qk_Q3]
            for tt in range(16 + 3):
                for k, st in enumerate(qk_stages):
                    j = tt - k
                    if 0 <= j < 16:
                        st(j)
            need_slot(base_slot + 23)

            def vp_W0(tc):
                vb_ = (0, 1) if tc % 2 == 0 else (2, 3)
                for vh in range(2):
                    for kc in range(DCH):
                        slot = base_slot + 16 + kc
                        S.op("P", lambda e, slot=slot, vb_=vb_, vh=vh, kc=kc, tc=tc: e.matmul(
                            pb[vb_[vh]][:, 0:512], xn[kc][:, tc * 128:(tc + 1) * 128],
                            rslot(slot)[:, vh * 512:(vh + 1) * 512],
                            start=(kc == 0), stop=(kc == DCH - 1)),
                            reads=[rname(slot), f"xn{kc}"], writes=[f"pb{vb_[vh]}"])

            def vp_W1(tc):
                vb_ = (0, 1) if tc % 2 == 0 else (2, 3)
                for vh in range(2):
                    S.op("A", lambda e, tc=tc, vb_=vb_, vh=vh: e.copy(
                        out=h[16 + tc][:, vh * 512:(vh + 1) * 512], in_=pb[vb_[vh]][:, 0:512]),
                        reads=[f"pb{vb_[vh]}"], writes=[f"h{16 + tc}"])

            for tt in range(8 + 1):
                if tt < 8:
                    vp_W0(tt)
                if tt >= 1:
                    vp_W1(tt - 1)
            def bh_T0(i):
                cp, r = divmod(i, 8)
                hh, b4 = divmod(r, 4)
                qbase, sb_ = hh * 64, i % 3
                for qc in range(2):
                    S.op("P", lambda e, sb_=sb_, qc=qc, cp=cp, qbase=qbase, b4=b4: e.matmul(
                        pb[sb_][:, qc * 256:(qc + 1) * 256],
                        h[cp][qbase:qbase + 64, b4 * 256 + qc * 128:b4 * 256 + (qc + 1) * 128],
                        h[8 + cp][qbase:qbase + 64, b4 * 256:(b4 + 1) * 256],
                        start=True, stop=True),
                        reads=[f"h{cp}", f"h{8 + cp}"], writes=[f"pb{sb_}"])

            def bh_T1(i):
                sb_, p2, p4 = i % 3, i % 4, i % 4
                S.op("V", lambda e, sb_=sb_, p2=p2: e.tensor_add(
                    smv[p2][:, 0:512], pb[sb_][:, 0:512], maskb[:, :]),
                    reads=[f"pb{sb_}", "maskb"], writes=[f"smv{p2}"])
                S.op("V", lambda e, p2=p2, p4=p4: e.reduce_max(
                    negm[p4][:, 0:2],
                    smv[p2][:, 0:512].rearrange("p (c n) -> p c n", c=2),
                    axis=AX.X, negate=True),
                    reads=[f"smv{p2}"], writes=[f"negm{p4}"], hazard=True)

            def bh_T2(i):
                p2, p3, p4 = i % 4, i % 4, i % 4
                for qc in range(2):
                    S.op("A", lambda e, p2=p2, p3=p3, p4=p4, qc=qc: e.activation(
                        out=ae[p3][:, qc * 256:(qc + 1) * 256],
                        in_=smv[p2][:, qc * 256:(qc + 1) * 256], func=AF.Exp,
                        bias=negm[p4][:, qc:qc + 1], scale=1.0,
                        accum_out=den4[p4][:, qc:qc + 1]),
                        reads=[f"smv{p2}", f"negm{p4}"],
                        writes=[f"ae{p3}", f"den4{p4}"], hazard=True)

            def bh_T3(i):
                p3, p4 = i % 4, i % 4
                S.op("V", lambda e, p4=p4: e.reciprocal(
                    rden4[p4][:, 0:4], den4[p4][:, 0:4]),
                    reads=[f"den4{p4}"], writes=[f"rden4{p4}"], hazard=True)
                for qc in range(2):
                    S.op("V", lambda e, p3=p3, p4=p4, qc=qc: e.tensor_scalar(
                        out=amn[p3][:, qc * 256:(qc + 1) * 256],
                        in0=ae[p3][:, qc * 256:(qc + 1) * 256],
                        scalar1=rden4[p4][:, qc:qc + 1], scalar2=None,
                        op0=ALU.mult),
                        reads=[f"ae{p3}", f"rden4{p4}"], writes=[f"amn{p3}"])

            def bh_T4(i):
                p3 = i % 4
                for kk in range(2):
                    for qc in range(2):
                        S.op("A", lambda e, kk=kk, qc=qc, p3=p3: e.dma_start(
                            out=atT[p3][:, kk * 256 + qc * 128:kk * 256 + (qc + 1) * 128],
                            in_=amn[p3][:, qc * 256 + kk * 128:qc * 256 + (kk + 1) * 128],
                            transpose=True),
                            reads=[f"amn{p3}"], writes=[f"atT{p3}"],
                            dma=True, semkey=f"T{p3}")

            def bh_T5(i):
                pass

            def bh_T6(i):
                cp, r = divmod(i, 8)
                hh, b4 = divmod(r, 4)
                hd, qbase = 2 * cp + hh, hh * 64
                ab_, p3 = 3 + (i % 4), i % 4
                for kk in range(2):
                    S.op("P", lambda e, ab_=ab_, b4=b4, kk=kk, hd=hd, p3=p3, qbase=qbase: e.matmul(
                        pb[ab_][qbase:qbase + 64, 0:256],
                        h[16 + b4 * 2 + kk][:, hd * 64:(hd + 1) * 64],
                        atT[p3][:, kk * 256:(kk + 1) * 256],
                        start=(kk == 0), stop=(kk == 1),
                        tile_position=(0, qbase)),
                        reads=[f"h{16 + b4 * 2 + kk}", f"atT{p3}"], writes=[f"pb{ab_}"])
                S.op("A", lambda e, ab_=ab_, cp=cp, qbase=qbase, b4=b4: e.copy(
                    out=ao[cp][qbase:qbase + 64, b4 * 256:(b4 + 1) * 256],
                    in_=pb[ab_][qbase:qbase + 64, 0:256]),
                    reads=[f"pb{ab_}"], writes=[f"ao{cp}"])

            bh_stages = [bh_T0, bh_T1, bh_T2, bh_T3, bh_T4, bh_T5, bh_T6]
            for tt in range(NBH + len(bh_stages) - 1):
                for k, st in enumerate(bh_stages):
                    i = tt - k
                    if 0 <= i < NBH:
                        st(i)
            for oc in range(DCH):
                slot = base_slot + 24 + oc
                need_slot(slot)
                ob_ = 4 + oc % 4
                for th in range(2):
                    for kc in range(DCH):
                        S.op("P", lambda e, slot=slot, ob_=ob_, th=th, kc=kc: e.matmul(
                            pb[ob_][:, 0:512], rslot(slot)[:, kc * 128:(kc + 1) * 128],
                            ao[kc][:, th * 512:(th + 1) * 512],
                            start=(kc == 0), stop=(kc == DCH - 1)),
                            reads=[rname(slot), f"ao{kc}"], writes=[f"pb{ob_}"])
                    S.op("V", lambda e, oc=oc, th=th, ob_=ob_: e.scalar_tensor_tensor(
                        out=x[oc][:, th * 512:(th + 1) * 512], in0=pb[ob_][:, 0:512],
                        scalar=1.0, in1=x[oc][:, th * 512:(th + 1) * 512],
                        op0=ALU.mult, op1=ALU.add),
                        reads=[f"pb{ob_}", f"x{oc}"], writes=[f"x{oc}"])

        for t in range(NT):
            for c in range(DCH):
                S.op("S", lambda e, c=c, t=t: e.dma_start(
                    out=x[c][:, :], in_=xin[c * 128:(c + 1) * 128, t * 1024:(t + 1) * 1024]),
                    reads=([f"xstb{c // 4}"] if t > 0 else []), writes=[f"x{c}"], dma=True,
                    semkey=("X" if c < 4 else "X2"))
            S.op("V", lambda e: e.engine_nop(), writes=[f"x{c}" for c in range(4)])
            S.op("V", lambda e: e.engine_nop(), writes=[f"x{c}" for c in range(4, DCH)])
            base = t * TILE_SLOTS
            if sublayers >= 1:
                emit_ln(t)
                emit_ffn(t, base)
            if sublayers >= 2:
                emit_ln(t)
                if debug_taps and debug_attn and t == 0:
                    for c in range(DCH):
                        S.op("S", lambda e, c=c: e.dma_start(out=dbg_xn[c], in_=xn[c][:, :]),
                             reads=[f"xn{c}"], writes=[f"dxn{c}"], dma=True)
                emit_attn(t, base + FFN_SLOTS)
                if debug_taps and debug_attn and t == 0:
                    for c in range(HCH):
                        S.op("S", lambda e, c=c: e.dma_start(out=dbg_h[c], in_=h[c][:, :]),
                             reads=[f"h{c}"], writes=[f"dh{c}"], dma=True)
                    for c in range(DCH):
                        S.op("S", lambda e, c=c: e.dma_start(out=dbg_ao[c], in_=ao[c][:, :]),
                             reads=[f"ao{c}"], writes=[f"dao{c}"], dma=True)
            if sublayers >= 3:
                if debug_taps and t == 0:
                    for c in range(DCH):
                        S.op("S", lambda e, c=c: e.dma_start(out=dbg_x2[c], in_=x[c][:, :]),
                             reads=[f"x{c}"], writes=[f"dx2{c}"], dma=True)
                emit_ln(t)
                if debug_taps and t == 0:
                    for c in range(DCH):
                        S.op("S", lambda e, c=c: e.dma_start(out=dbg_xn3[c], in_=xn[c][:, :]),
                             reads=[f"xn{c}"], writes=[f"dxn3{c}"], dma=True)
                emit_ffn(t, base + FFN_SLOTS + ATTN_SLOTS, store_x=True)
                if debug_taps and t == 0:
                    for c in range(HCH):
                        S.op("S", lambda e, c=c: e.dma_start(out=dbg_h3[c], in_=h[c][:, :]),
                             reads=[f"h{c}"], writes=[f"dh3{c}"], dma=True)
            else:
                state["loaded"] = (t + 1) * TILE_SLOTS
            if sublayers < 3:
                for c in range(DCH):
                    S.op("G", lambda e, c=c, t=t: e.dma_start(
                        out=xout[c * 128:(c + 1) * 128, t * 1024:(t + 1) * 1024], in_=x[c][:, :]),
                        reads=[f"x{c}"], writes=[f"xo{c}"], dma=True)
                for pas in range(2):
                    S.op("G", lambda e: e.engine_nop(),
                         reads=[f"xo{pas * 4 + oc}" for oc in range(4)],
                         writes=[f"xstb{pas}"])

        S.emit(block, sems)
    return nc


def _host_inputs(L0_attn_x, L0_attn_Wqkv, L0_attn_Wout, L0_ffn_w_gate,
                 L0_ffn_w_up, L0_ffn_w_down):
    xfull = np.ascontiguousarray(np.asarray(L0_attn_x, dtype=np.float32)).reshape(B * L, D)
    wqkv = np.asarray(L0_attn_Wqkv, dtype=np.float32)
    wout = np.asarray(L0_attn_Wout, dtype=np.float32)
    wg = np.asarray(L0_ffn_w_gate)
    wu = np.asarray(L0_ffn_w_up)
    wd = np.asarray(L0_ffn_w_down)

    def lhsT_slots(w):
        return np.ascontiguousarray(
            w.reshape(8, 128, 8, 128).transpose(0, 3, 2, 1).reshape(8, 128, 1024))

    wg_s = np.ascontiguousarray(
        wg.reshape(HCH, 128, 8, 128).transpose(0, 3, 2, 1).reshape(HCH, 128, 1024))
    wu_s = np.ascontiguousarray(
        wu.reshape(HCH, 128, 8, 128).transpose(0, 3, 2, 1).reshape(HCH, 128, 1024))
    gu = np.stack([wg_s, wu_s], axis=1).reshape(2 * HCH, 128, 1024)
    wd6 = wd.reshape(2, 4, 128, 12, 2, 128).transpose(0, 3, 5, 4, 1, 2)
    wd_s = np.ascontiguousarray(wd6.reshape(24, 128, 1024))
    ffn_stream = np.concatenate([gu, wd_s], axis=0).astype(NBF)
    assert ffn_stream.shape == (FFN_SLOTS, 128, 1024)

    pj = np.concatenate([np.arange(0, 64, 2), np.arange(1, 64, 2)])
    perm_rows = (np.arange(1024) // 64) * 64 + pj[np.arange(1024) % 64]
    wq_p = wqkv[0:1024][perm_rows]
    wk_p = wqkv[1024:2048][perm_rows]
    wv = wqkv[2048:3072]
    vs = np.ascontiguousarray(wv.reshape(1024, 8, 128).transpose(1, 2, 0))
    attn_stream = np.concatenate(
        [lhsT_slots(wq_p), lhsT_slots(wk_p), vs, lhsT_slots(wout)],
        axis=0).astype(np.float32)
    assert attn_stream.shape == (ATTN_SLOTS, 128, 1024)

    pvec = np.arange(128)
    inv_freq = 1.0 / 10000.0 ** (np.arange(0, HDIM, 2, dtype=np.float64) / HDIM)
    sign = np.where((pvec % 64) < 32, -1.0, 1.0)
    mask = np.zeros((128, 512), dtype=np.float32)
    for qc in range(2):
        qloc = qc * 128 + pvec
        mask[:, qc * 256:(qc + 1) * 256] = np.where(
            np.arange(256)[None, :] <= qloc[:, None], 0.0, -1e30)
    identf = np.eye(128, dtype=np.float32).astype(NBF)
    permm = np.zeros((128, 128), dtype=np.float32)
    src = (pvec // 64) * 64 + ((pvec % 64) + 32) % 64
    permm[src, pvec] = 1.0
    permm = permm.astype(NBF)

    in_maps = []
    for core in range(NCORES):
        rs = core * TPC
        posn = (rs % L) + np.arange(TPC, dtype=np.float64)
        theta = posn[None, :] * inv_freq[pvec % 32][:, None]
        cos_t = np.cos(theta).astype(np.float32)
        sin_t = (np.sin(theta) * sign[:, None]).astype(np.float32)
        xcore = np.ascontiguousarray(xfull[rs:rs + TPC].T)
        in_maps.append({
            "xin": xcore, "ffn_w": ffn_stream, "attn_w": attn_stream,
            "cos_t": cos_t, "sin_t": sin_t, "mask_t": mask,
            "idn_t": identf, "perm_t": permm,
        })
    return in_maps


def kernel(L0_attn_x, L0_attn_Wqkv, L0_attn_Wout, L0_ffn_w_gate,
           L0_ffn_w_up, L0_ffn_w_down):
    in_maps = _host_inputs(L0_attn_x, L0_attn_Wqkv, L0_attn_Wout,
                           L0_ffn_w_gate, L0_ffn_w_up, L0_ffn_w_down)
    nc = build_program(sublayers=_SUBLAYERS, debug_taps=_DEBUG_TAPS, debug_attn=_DEBUG_ATTN)
    res = run_bass_kernel_spmd(nc, in_maps, core_ids=list(range(NCORES)))
    if _DEBUG_TAPS:
        global _DEBUG_RES
        _DEBUG_RES = res.results
    shards = [np.asarray(r["xout"]).T for r in res.results]
    return np.concatenate(shards, axis=0).reshape(B, L, D)


# version 09
# speedup vs baseline: 1.4803x; 1.4803x over previous
from contextlib import ExitStack

import numpy as np
import ml_dtypes
import concourse.bass as bass
import concourse.mybir as mybir
from concourse.bass_utils import run_bass_kernel_spmd

F32 = mybir.dt.float32
BF16 = mybir.dt.bfloat16
AF = mybir.ActivationFunctionType
ALU = mybir.AluOpType
AX = mybir.AxisListType
NBF = ml_dtypes.bfloat16

B, L, D = 2, 16384, 1024
HID, NH, HDIM, BS = 3072, 16, 64, 256
NCORES = 8
TPC = (B * L) // NCORES
TILE = 1024
NT = TPC // TILE
DCH = D // 128
HCH = HID // 128
SD = 0.5 / (127.0 * 127.0)
NBH = 4 * NH

_SUBLAYERS = 3

FFN_SLOTS = 72
ATTN_SLOTS = 32
TILE_SLOTS = FFN_SLOTS + ATTN_SLOTS + FFN_SLOTS


class Op:
    __slots__ = ("eng", "fn", "dma", "deps", "needs_inc", "cum", "semkey",
                 "hazard")

    def __init__(self, eng, fn, dma):
        self.eng, self.fn, self.dma = eng, fn, dma
        self.deps = []
        self.needs_inc = False
        self.cum = None
        self.hazard = False
        self.semkey = "D" if (eng == "G" and dma) else eng


class Sched:

    def __init__(self):
        self.ops = {e: [] for e in "SPVAG"}
        self.bufs = {}

    def _buf(self, name):
        b = self.bufs.get(name)
        if b is None:
            b = {"w": None, "r": []}
            self.bufs[name] = b
        return b

    def op(self, eng, fn, reads=(), writes=(), dma=False, semkey=None,
           hazard=False):
        o = Op(eng, fn, dma)
        o.hazard = hazard
        if semkey is not None:
            o.semkey = semkey
        deps = set()
        for n in reads:
            b = self._buf(n)
            if b["w"] is not None:
                deps.add(b["w"])
        for n in writes:
            b = self._buf(n)
            if b["w"] is not None:
                deps.add(b["w"])
            deps.update(b["r"])
        for n in reads:
            self._buf(n)["r"].append(o)
        for n in writes:
            b = self._buf(n)
            b["w"] = o
            b["r"] = []
        for d in deps:
            if d is o:
                continue
            if d.eng == eng and not d.dma and not d.hazard:
                continue
            o.deps.append(d)
            d.needs_inc = True
        self.ops[eng].append(o)
        return o

    SEMKEYS = tuple("SPVAGD") + ("K0", "K1", "K2", "C0", "C1", "X", "X2",
                                 "R0", "R1", "R2", "R3",
                                 "T0", "T1", "T2", "T3")

    def emit(self, block, sems):
        totals = {k: 0 for k in self.SEMKEYS}
        for e, lst in self.ops.items():
            for o in lst:
                if o.needs_inc or o.dma:
                    o.needs_inc = True
                    totals[o.semkey] += 16 if o.dma else 1
                o.cum = totals[o.semkey]

        def body_for(e):
            def body(eng):
                hw = {k: 0 for k in totals}
                for o in self.ops[e]:
                    need = {}
                    for d in o.deps:
                        if d.cum > need.get(d.semkey, 0):
                            need[d.semkey] = d.cum
                    for pe, v in sorted(need.items()):
                        if v > hw[pe]:
                            eng.wait_ge(sems[pe], v)
                            hw[pe] = v
                    inst = o.fn(eng)
                    if o.needs_inc:
                        inst.then_inc(sems[o.semkey], 16 if o.dma else 1)
                if e == "S":
                    for k in ("S", "K0", "K1", "K2", "C0", "C1", "X", "X2",
                              "R0", "R1", "R2", "R3"):
                        if totals[k] > 0:
                            eng.wait_ge(sems[k], totals[k])
                if e == "A":
                    for k in ("T0", "T1", "T2", "T3"):
                        if totals[k] > 0:
                            eng.wait_ge(sems[k], totals[k])
                if e == "G" and totals["D"] > 0:
                    eng.wait_ge(sems["D"], totals["D"])
            return body

        block.sync(body_for("S"))
        block.tensor(body_for("P"))
        block.vector(body_for("V"))
        block.scalar(body_for("A"))
        block.gpsimd(body_for("G"))


_DEBUG_TAPS = False
_DEBUG_ATTN = True


def build_program(sublayers=3, debug_taps=False, debug_attn=True):
    nc = bass.Bass("TRN2", target_bir_lowering=False)

    xin = nc.dram_tensor("xin", [D, TPC], F32, kind="ExternalInput").ap()
    if debug_taps:
        dbg_h = nc.dram_tensor("dbg_h", [HCH, 128, 1024], BF16, kind="ExternalOutput").ap()
        dbg_ao = nc.dram_tensor("dbg_ao", [DCH, 128, 1024], BF16, kind="ExternalOutput").ap()
        dbg_xn = nc.dram_tensor("dbg_xn", [DCH, 128, 1024], BF16, kind="ExternalOutput").ap()
        dbg_x2 = nc.dram_tensor("dbg_x2", [DCH, 128, 1024], F32, kind="ExternalOutput").ap()
        dbg_xn3 = nc.dram_tensor("dbg_xn3", [DCH, 128, 1024], BF16, kind="ExternalOutput").ap()
        dbg_h3 = nc.dram_tensor("dbg_h3", [HCH, 128, 1024], BF16, kind="ExternalOutput").ap()
    xout = nc.dram_tensor("xout", [D, TPC], F32, kind="ExternalOutput").ap()
    ffn_w = nc.dram_tensor("ffn_w", [FFN_SLOTS, 128, 1024], BF16, kind="ExternalInput").ap()
    attn_w = nc.dram_tensor("attn_w", [ATTN_SLOTS, 128, 1024], F32, kind="ExternalInput").ap()
    cos_t = nc.dram_tensor("cos_t", [128, TPC], F32, kind="ExternalInput").ap()
    sin_t = nc.dram_tensor("sin_t", [128, TPC], F32, kind="ExternalInput").ap()
    mask_t = nc.dram_tensor("mask_t", [128, 512], F32, kind="ExternalInput").ap()
    idn_t = nc.dram_tensor("idn_t", [128, 128], BF16, kind="ExternalInput").ap()
    perm_t = nc.dram_tensor("perm_t", [128, 128], BF16, kind="ExternalInput").ap()
    attn_wb = nc.dram_tensor("attn_wb", [ATTN_SLOTS, 128, 1024], BF16, kind="Internal").ap()

    with ExitStack() as ctx:
        def sb(name, shape, dt):
            return ctx.enter_context(nc.sbuf_tensor(name, shape, dt))

        x = [sb(f"x{c}", [128, 1024], F32) for c in range(DCH)]
        xn = [sb(f"xn{c}", [128, 1024], BF16) for c in range(DCH)]
        h = [sb(f"h{c}", [128, 1024], BF16) for c in range(HCH)]
        ao = [sb(f"ao{c}", [128, 1024], BF16) for c in range(DCH)]
        ring = sb("ring", [128, 8 * 1024], BF16)
        cosb = sb("cosb", [128, 1024], F32)
        sinb = sb("sinb", [128, 1024], F32)
        ts_ = [sb(f"t{i}", [128, 1024], F32) for i in range(4)]
        bs = [sb(f"bs{i}", [128, 1024], BF16) for i in range(2)]
        sgb = [sb(f"sgb{i}", [128, 512], BF16) for i in range(2)]
        smv = [sb(f"smv{i}", [128, 512], F32) for i in range(4)]
        ae = [sb(f"ae{i}", [128, 512], BF16) for i in range(4)]
        amn = [sb(f"amn{i}", [128, 512], BF16) for i in range(4)]
        atT = [sb(f"atT{i}", [128, 512], BF16) for i in range(4)]
        negm = [sb(f"negm{i}", [128, 2], F32) for i in range(4)]
        den4 = [sb(f"den4{i}", [128, 4], F32) for i in range(4)]
        rden4 = [sb(f"rden4{i}", [128, 4], F32) for i in range(4)]
        rowsA = sb("rowsA", [1, 1024], F32)
        rowsB = sb("rowsB", [1, 1024], F32)
        rowsC = sb("rowsC", [1, 1024], F32)
        maskb = sb("maskb", [128, 512], F32)
        identb = sb("identb", [128, 128], BF16)
        permb = sb("permb", [128, 128], BF16)
        onesc = sb("onesc", [128, 1], BF16)
        onesr = sb("onesr", [1, 128], F32)
        epsb = sb("epsb", [128, 1], F32)
        pb = [ctx.enter_context(nc.psum_tensor(f"pb{i}", [128, 512], F32)) for i in range(8)]

        sems = {e: ctx.enter_context(nc.semaphore(f"sem{e}")) for e in Sched.SEMKEYS}
        block = ctx.enter_context(nc.Block())

        S = Sched()

        def rname(i):
            return f"ring{i % 8}"

        def rslot(i):
            j = i % 8
            return ring[:, j * 1024:(j + 1) * 1024]

        S.op("S", lambda e: e.dma_start(out=maskb[:, :], in_=mask_t), writes=["maskb"],
             dma=True, semkey="K0")
        S.op("S", lambda e: e.dma_start(out=identb[:, :], in_=idn_t), writes=["identb"],
             dma=True, semkey="K1")
        S.op("S", lambda e: e.dma_start(out=permb[:, :], in_=perm_t), writes=["permb"],
             dma=True, semkey="K2")
        S.op("V", lambda e: e.memset(onesc[:, :], 1.0), writes=["onesc"])
        S.op("V", lambda e: e.memset(onesr[:, :], 1.0), writes=["onesr"])
        S.op("V", lambda e: e.memset(epsb[:, :], 1e-5), writes=["epsb"])
        for i in range(4):
            S.op("V", lambda e, i=i: e.memset(den4[i][:, :], 1.0), writes=[f"den4{i}"])
        for j in range(0, ATTN_SLOTS, 4):
            S.op("G", lambda e, j=j: e.dma_start(out=attn_wb[j:j + 4], in_=attn_w[j:j + 4]),
                 writes=[f"asb{k}" for k in range(j, j + 4)], dma=True)
        S.op("G", lambda e: e.engine_nop(),
             reads=[f"asb{k}" for k in range(ATTN_SLOTS)], writes=["asball"])

        state = {"loaded": 0}

        def src_of(slot):
            s = slot % TILE_SLOTS
            if s < FFN_SLOTS:
                return ffn_w, s, []
            if s < FFN_SLOTS + ATTN_SLOTS:
                j = s - FFN_SLOTS
                return attn_wb, j, ["asball"]
            return ffn_w, s - FFN_SLOTS - ATTN_SLOTS, []

        def need_slot(gi):
            while state["loaded"] <= gi:
                i = state["loaded"]
                arr, j, rd = src_of(i)
                arr2, j2, rd2 = src_of(i + 1)
                assert arr is arr2 and j2 == j + 1 and i % 2 == 0
                pos = i % 8
                out_ap = ring[:, pos * 1024:(pos + 2) * 1024].rearrange(
                    "p (s j) -> p s j", s=2)
                in_ap = arr[j:j + 2].rearrange("s p j -> p s j")
                S.op("S", lambda e, o=out_ap, i_=in_ap: e.dma_start(out=o, in_=i_),
                     reads=rd + rd2, writes=[rname(i), rname(i + 1)], dma=True,
                     semkey=f"R{(i % 8) // 2}")
                state["loaded"] += 2

        def emit_ln(t):
            for c in range(DCH):
                S.op("A", lambda e, c=c: e.copy(out=xn[c][:, :], in_=x[c][:, :]),
                     reads=[f"x{c}"], writes=[f"xn{c}"])
                p = c % 2
                S.op("A", lambda e, c=c, p=p: e.activation(
                    out=bs[p][:, :], in_=xn[c][:, :], func=AF.Square),
                    reads=[f"xn{c}"], writes=[f"bs{p}"])
                for hf in range(2):
                    S.op("P", lambda e, c=c, hf=hf: e.matmul(
                        pb[4 + hf][0:1, 0:512], onesc[:, :],
                        xn[c][:, hf * 512:(hf + 1) * 512],
                        start=(c == 0), stop=(c == DCH - 1)),
                        reads=[f"xn{c}", "onesc"], writes=[f"pb{4 + hf}"])
                    S.op("P", lambda e, c=c, p=p, hf=hf: e.matmul(
                        pb[6 + hf][0:1, 0:512], onesc[:, :],
                        bs[p][:, hf * 512:(hf + 1) * 512],
                        start=(c == 0), stop=(c == DCH - 1)),
                        reads=[f"bs{p}", "onesc"], writes=[f"pb{6 + hf}"])
            for hf in range(2):
                S.op("V", lambda e, hf=hf: e.tensor_scalar(
                    out=rowsA[0:1, hf * 512:(hf + 1) * 512], in0=pb[4 + hf][0:1, 0:512],
                    scalar1=-1.0 / D, scalar2=None, op0=ALU.mult),
                    reads=[f"pb{4 + hf}"], writes=["rowsA"])
            for hf in range(2):
                S.op("P", lambda e, hf=hf: e.matmul(
                    pb[2 + hf][:, 0:512], onesr[0:1, :], rowsA[0:1, hf * 512:(hf + 1) * 512],
                    start=True, stop=True),
                    reads=["rowsA", "onesr"], writes=[f"pb{2 + hf}"])
                S.op("A", lambda e, hf=hf: e.copy(
                    out=ts_[1][:, hf * 512:(hf + 1) * 512], in_=pb[2 + hf][:, 0:512]),
                    reads=[f"pb{2 + hf}"], writes=["t1"])
            for hf in range(2):
                S.op("A", lambda e, hf=hf: e.activation(
                    out=rowsC[0:1, hf * 512:(hf + 1) * 512], in_=pb[4 + hf][0:1, 0:512],
                    func=AF.Square),
                    reads=[f"pb{4 + hf}"], writes=["rowsC"])
                S.op("V", lambda e, hf=hf: e.scalar_tensor_tensor(
                    out=rowsB[0:1, hf * 512:(hf + 1) * 512],
                    in0=rowsC[0:1, hf * 512:(hf + 1) * 512], scalar=-1.0 / D,
                    in1=pb[6 + hf][0:1, 0:512], op0=ALU.mult, op1=ALU.add),
                    reads=["rowsC", f"pb{6 + hf}"], writes=["rowsB"])
            S.op("A", lambda e: e.activation(
                out=rowsB[0:1, 0:1024], in_=rowsB[0:1, 0:1024], func=AF.Ln,
                scale=1.0 / D, bias=epsb[0:1, 0:1]),
                reads=["rowsB", "epsb"], writes=["rowsB"])
            S.op("A", lambda e: e.activation(
                out=rowsC[0:1, 0:1024], in_=rowsB[0:1, 0:1024], func=AF.Exp,
                scale=-0.5),
                reads=["rowsB"], writes=["rowsC"])
            for hf in range(2):
                S.op("P", lambda e, hf=hf: e.matmul(
                    pb[hf][:, 0:512], onesr[0:1, :], rowsC[0:1, hf * 512:(hf + 1) * 512],
                    start=True, stop=True),
                    reads=["rowsC", "onesr"], writes=[f"pb{hf}"])
                S.op("A", lambda e, hf=hf: e.copy(
                    out=ts_[0][:, hf * 512:(hf + 1) * 512], in_=pb[hf][:, 0:512]),
                    reads=[f"pb{hf}"], writes=["t0"])
            for c in range(DCH):
                tp = 2 + (c % 2)
                S.op("G", lambda e, c=c, tp=tp: e.tensor_add(
                    ts_[tp][:, :], x[c][:, :], ts_[1][:, :]),
                    reads=[f"x{c}", "t1"], writes=[f"t{tp}"])
                S.op("V", lambda e, c=c, tp=tp: e.tensor_mul(
                    xn[c][:, :], ts_[tp][:, :], ts_[0][:, :]),
                    reads=[f"t{tp}", "t0"], writes=[f"xn{c}"])

        def emit_ffn(t, base_slot, store_x=False):
            def gu_F0(p):
                hc, th = divmod(p, 2)
                gslot, uslot = base_slot + 2 * hc, base_slot + 2 * hc + 1
                need_slot(uslot)
                gb_, ub_ = (0, 4) if hc % 2 == 0 else (2, 6)
                for kc in range(DCH):
                    S.op("P", lambda e, gslot=gslot, gb_=gb_, th=th, kc=kc: e.matmul(
                        pb[gb_ + th][:, 0:512], rslot(gslot)[:, kc * 128:(kc + 1) * 128],
                        xn[kc][:, th * 512:(th + 1) * 512],
                        start=(kc == 0), stop=(kc == DCH - 1)),
                        reads=[rname(gslot), f"xn{kc}"], writes=[f"pb{gb_ + th}"])
                for kc in range(DCH):
                    S.op("P", lambda e, uslot=uslot, ub_=ub_, th=th, kc=kc: e.matmul(
                        pb[ub_ + th][:, 0:512], rslot(uslot)[:, kc * 128:(kc + 1) * 128],
                        xn[kc][:, th * 512:(th + 1) * 512],
                        start=(kc == 0), stop=(kc == DCH - 1)),
                        reads=[rname(uslot), f"xn{kc}"], writes=[f"pb{ub_ + th}"])

            def gu_F1(p):
                hc, th = divmod(p, 2)
                gb_ = 0 if hc % 2 == 0 else 2
                S.op("A", lambda e, gb_=gb_, th=th: e.activation(
                    out=sgb[th][:, :], in_=pb[gb_ + th][:, 0:512], func=AF.Silu,
                    scale=1.0 / 127.0),
                    reads=[f"pb{gb_ + th}"], writes=[f"sgb{th}"])

            def gu_F2(p):
                hc, th = divmod(p, 2)
                ub_ = 4 if hc % 2 == 0 else 6
                S.op("V", lambda e, hc=hc, ub_=ub_, th=th: e.tensor_mul(
                    h[hc][:, th * 512:(th + 1) * 512], sgb[th][:, :],
                    pb[ub_ + th][:, 0:512]),
                    reads=[f"sgb{th}", f"pb{ub_ + th}"], writes=[f"h{hc}"])

            gu_stages = [gu_F0, gu_F1, gu_F2]
            for tt in range(2 * HCH + 2):
                for k, st in enumerate(gu_stages):
                    p = tt - k
                    if 0 <= p < 2 * HCH:
                        st(p)
            for pas in range(2):
                for hc in range(HCH):
                    slot = base_slot + 48 + pas * 12 + hc // 2
                    need_slot(slot)
                    for oc in range(4):
                        off = (((hc % 2) * 4) + oc) * 128
                        for th in range(2):
                            S.op("P", lambda e, slot=slot, off=off, oc=oc, th=th, hc=hc: e.matmul(
                                pb[oc * 2 + th][:, 0:512],
                                rslot(slot)[:, off:off + 128],
                                h[hc][:, th * 512:(th + 1) * 512],
                                start=(hc == 0), stop=(hc == HCH - 1)),
                                reads=[rname(slot), f"h{hc}"], writes=[f"pb{oc * 2 + th}"])
                for oc in range(4):
                    xc = pas * 4 + oc
                    for th in range(2):
                        S.op("V", lambda e, oc=oc, th=th, xc=xc: e.scalar_tensor_tensor(
                            out=x[xc][:, th * 512:(th + 1) * 512],
                            in0=pb[oc * 2 + th][:, 0:512], scalar=SD,
                            in1=x[xc][:, th * 512:(th + 1) * 512],
                            op0=ALU.mult, op1=ALU.add),
                            reads=[f"pb{oc * 2 + th}", f"x{xc}"], writes=[f"x{xc}"])
                if store_x:
                    for oc in range(4):
                        xc = pas * 4 + oc
                        S.op("G", lambda e, xc=xc, t=t: e.dma_start(
                            out=xout[xc * 128:(xc + 1) * 128, t * 1024:(t + 1) * 1024],
                            in_=x[xc][:, :]),
                            reads=[f"x{xc}"], writes=[f"xo{xc}"], dma=True)
                    S.op("G", lambda e: e.engine_nop(),
                         reads=[f"xo{pas * 4 + oc}" for oc in range(4)],
                         writes=[f"xstb{pas}"])

        def emit_attn(t, base_slot):
            tc0 = t * 1024
            S.op("S", lambda e, t=t: e.dma_start(
                out=cosb[:, :], in_=cos_t[:, t * 1024:(t + 1) * 1024]),
                writes=["cosb"], dma=True, semkey="C0")
            S.op("S", lambda e, t=t: e.dma_start(
                out=sinb[:, :], in_=sin_t[:, t * 1024:(t + 1) * 1024]),
                writes=["sinb"], dma=True, semkey="C1")
            def qk_Q0(j):
                qk, oc = divmod(j, DCH)
                slot = base_slot + qk * 8 + oc
                need_slot(slot)
                qb_ = (0, 1) if j % 2 == 0 else (2, 3)
                for th in range(2):
                    for kc in range(DCH):
                        S.op("P", lambda e, slot=slot, qb_=qb_, th=th, kc=kc: e.matmul(
                            pb[qb_[th]][:, 0:512], rslot(slot)[:, kc * 128:(kc + 1) * 128],
                            xn[kc][:, th * 512:(th + 1) * 512],
                            start=(kc == 0), stop=(kc == DCH - 1)),
                            reads=[rname(slot), f"xn{kc}"], writes=[f"pb{qb_[th]}"])

            def qk_Q1(j):
                qk = j // DCH
                par = j % 2
                qb_ = (0, 1) if par == 0 else (2, 3)
                sw_ = (4, 5) if par == 0 else (6, 7)
                for th in range(2):
                    S.op("A", lambda e, qb_=qb_, th=th, par=par, qk=qk: e.activation(
                        out=bs[par][:, th * 512:(th + 1) * 512], in_=pb[qb_[th]][:, 0:512],
                        func=AF.Copy, scale=(0.125 if qk == 0 else 1.0)),
                        reads=[f"pb{qb_[th]}"], writes=[f"bs{par}"])
                    S.op("P", lambda e, sw_=sw_, th=th, par=par: e.matmul(
                        pb[sw_[th]][:, 0:512], permb[:, :],
                        bs[par][:, th * 512:(th + 1) * 512], start=True, stop=True),
                        reads=["permb", f"bs{par}"], writes=[f"pb{sw_[th]}"])

            def qk_Q2(j):
                par = j % 2
                sw_ = (4, 5) if par == 0 else (6, 7)
                tA, tB = 2 + par, 0 + par
                S.op("V", lambda e, par=par, tA=tA: e.tensor_mul(
                    ts_[tA][:, :], bs[par][:, :], cosb[:, 0:1024]),
                    reads=[f"bs{par}", "cosb"], writes=[f"t{tA}"])
                for th in range(2):
                    S.op("V", lambda e, sw_=sw_, th=th, tB=tB: e.tensor_mul(
                        ts_[tB][:, th * 512:(th + 1) * 512], pb[sw_[th]][:, 0:512],
                        sinb[:, th * 512:(th + 1) * 512]),
                        reads=[f"pb{sw_[th]}", "sinb"], writes=[f"t{tB}"])

            def qk_Q3(j):
                par = j % 2
                tA, tB = 2 + par, 0 + par
                S.op("G", lambda e, hdst=j, tA=tA, tB=tB: e.tensor_add(
                    h[hdst][:, :], ts_[tA][:, :], ts_[tB][:, :]),
                    reads=[f"t{tA}", f"t{tB}"], writes=[f"h{j}"])

            qk_stages = [qk_Q0, qk_Q1, qk_Q2, qk_Q3]
            for tt in range(16 + 3):
                for k, st in enumerate(qk_stages):
                    j = tt - k
                    if 0 <= j < 16:
                        st(j)
            need_slot(base_slot + 23)

            def vp_W0(tc):
                vb_ = (0, 1) if tc % 2 == 0 else (2, 3)
                for vh in range(2):
                    for kc in range(DCH):
                        slot = base_slot + 16 + kc
                        S.op("P", lambda e, slot=slot, vb_=vb_, vh=vh, kc=kc, tc=tc: e.matmul(
                            pb[vb_[vh]][:, 0:512], xn[kc][:, tc * 128:(tc + 1) * 128],
                            rslot(slot)[:, vh * 512:(vh + 1) * 512],
                            start=(kc == 0), stop=(kc == DCH - 1)),
                            reads=[rname(slot), f"xn{kc}"], writes=[f"pb{vb_[vh]}"])

            def vp_W1(tc):
                vb_ = (0, 1) if tc % 2 == 0 else (2, 3)
                for vh in range(2):
                    S.op("A", lambda e, tc=tc, vb_=vb_, vh=vh: e.copy(
                        out=h[16 + tc][:, vh * 512:(vh + 1) * 512], in_=pb[vb_[vh]][:, 0:512]),
                        reads=[f"pb{vb_[vh]}"], writes=[f"h{16 + tc}"])

            for tt in range(8 + 1):
                if tt < 8:
                    vp_W0(tt)
                if tt >= 1:
                    vp_W1(tt - 1)
            def bh_T0(i):
                cp, r = divmod(i, 8)
                hh, b4 = divmod(r, 4)
                qbase, sb_ = hh * 64, i % 3
                for qc in range(2):
                    S.op("P", lambda e, sb_=sb_, qc=qc, cp=cp, qbase=qbase, b4=b4: e.matmul(
                        pb[sb_][:, qc * 256:(qc + 1) * 256],
                        h[cp][qbase:qbase + 64, b4 * 256 + qc * 128:b4 * 256 + (qc + 1) * 128],
                        h[8 + cp][qbase:qbase + 64, b4 * 256:(b4 + 1) * 256],
                        start=True, stop=True),
                        reads=[f"h{cp}", f"h{8 + cp}"], writes=[f"pb{sb_}"])

            def bh_T1(i):
                sb_, p2, p4 = i % 3, i % 4, i % 4
                S.op("V", lambda e, sb_=sb_, p2=p2: e.tensor_add(
                    smv[p2][:, 0:512], pb[sb_][:, 0:512], maskb[:, :]),
                    reads=[f"pb{sb_}", "maskb"], writes=[f"smv{p2}"])
                S.op("V", lambda e, p2=p2, p4=p4: e.reduce_max(
                    negm[p4][:, 0:2],
                    smv[p2][:, 0:512].rearrange("p (c n) -> p c n", c=2),
                    axis=AX.X, negate=True),
                    reads=[f"smv{p2}"], writes=[f"negm{p4}"], hazard=True)

            def bh_T2(i):
                p2, p3, p4 = i % 4, i % 4, i % 4
                for qc in range(2):
                    S.op("A", lambda e, p2=p2, p3=p3, p4=p4, qc=qc: e.activation(
                        out=ae[p3][:, qc * 256:(qc + 1) * 256],
                        in_=smv[p2][:, qc * 256:(qc + 1) * 256], func=AF.Exp,
                        bias=negm[p4][:, qc:qc + 1], scale=1.0,
                        accum_out=den4[p4][:, qc:qc + 1]),
                        reads=[f"smv{p2}", f"negm{p4}"],
                        writes=[f"ae{p3}", f"den4{p4}"], hazard=True)

            def bh_T3(i):
                p3, p4 = i % 4, i % 4
                S.op("V", lambda e, p4=p4: e.reciprocal(
                    rden4[p4][:, 0:4], den4[p4][:, 0:4]),
                    reads=[f"den4{p4}"], writes=[f"rden4{p4}"], hazard=True)
                for qc in range(2):
                    S.op("V", lambda e, p3=p3, p4=p4, qc=qc: e.tensor_scalar(
                        out=amn[p3][:, qc * 256:(qc + 1) * 256],
                        in0=ae[p3][:, qc * 256:(qc + 1) * 256],
                        scalar1=rden4[p4][:, qc:qc + 1], scalar2=None,
                        op0=ALU.mult),
                        reads=[f"ae{p3}", f"rden4{p4}"], writes=[f"amn{p3}"])

            def bh_T4(i):
                tb_, p3 = 3 + (i % 2), i % 4
                tbf = pb[tb_][:, 0:256].bitcast(BF16)
                for kk in range(2):
                    for qc in range(2):
                        S.op("P", lambda e, tbf=tbf, kk=kk, qc=qc, p3=p3: e.transpose(
                            tbf[:, kk * 256 + qc * 128:kk * 256 + (qc + 1) * 128],
                            amn[p3][:, qc * 256 + kk * 128:qc * 256 + (kk + 1) * 128],
                            identb[:, :]),
                            reads=[f"amn{p3}", "identb"], writes=[f"pb{tb_}"])

            def bh_T5(i):
                tb_, p3 = 3 + (i % 2), i % 4
                tbf = pb[tb_][:, 0:256].bitcast(BF16)
                S.op("A", lambda e, tbf=tbf, p3=p3: e.copy(
                    out=atT[p3][:, :], in_=tbf[:, 0:512]),
                    reads=[f"pb{tb_}"], writes=[f"atT{p3}"])

            def bh_T6(i):
                cp, r = divmod(i, 8)
                hh, b4 = divmod(r, 4)
                hd, qbase = 2 * cp + hh, hh * 64
                ab_, p3 = 5 + (i % 2), i % 4
                for kk in range(2):
                    S.op("P", lambda e, ab_=ab_, b4=b4, kk=kk, hd=hd, p3=p3, qbase=qbase: e.matmul(
                        pb[ab_][qbase:qbase + 64, 0:256],
                        h[16 + b4 * 2 + kk][:, hd * 64:(hd + 1) * 64],
                        atT[p3][:, kk * 256:(kk + 1) * 256],
                        start=(kk == 0), stop=(kk == 1),
                        tile_position=(0, qbase)),
                        reads=[f"h{16 + b4 * 2 + kk}", f"atT{p3}"], writes=[f"pb{ab_}"])
                S.op("A", lambda e, ab_=ab_, cp=cp, qbase=qbase, b4=b4: e.copy(
                    out=ao[cp][qbase:qbase + 64, b4 * 256:(b4 + 1) * 256],
                    in_=pb[ab_][qbase:qbase + 64, 0:256]),
                    reads=[f"pb{ab_}"], writes=[f"ao{cp}"])

            bh_stages = [bh_T0, bh_T1, bh_T2, bh_T3, bh_T4, bh_T5, bh_T6]
            for tt in range(NBH + len(bh_stages) - 1):
                for k, st in enumerate(bh_stages):
                    i = tt - k
                    if 0 <= i < NBH:
                        st(i)
            for oc in range(DCH):
                slot = base_slot + 24 + oc
                need_slot(slot)
                ob_ = 4 + oc % 4
                for th in range(2):
                    for kc in range(DCH):
                        S.op("P", lambda e, slot=slot, ob_=ob_, th=th, kc=kc: e.matmul(
                            pb[ob_][:, 0:512], rslot(slot)[:, kc * 128:(kc + 1) * 128],
                            ao[kc][:, th * 512:(th + 1) * 512],
                            start=(kc == 0), stop=(kc == DCH - 1)),
                            reads=[rname(slot), f"ao{kc}"], writes=[f"pb{ob_}"])
                    S.op("V", lambda e, oc=oc, th=th, ob_=ob_: e.scalar_tensor_tensor(
                        out=x[oc][:, th * 512:(th + 1) * 512], in0=pb[ob_][:, 0:512],
                        scalar=1.0, in1=x[oc][:, th * 512:(th + 1) * 512],
                        op0=ALU.mult, op1=ALU.add),
                        reads=[f"pb{ob_}", f"x{oc}"], writes=[f"x{oc}"])

        for t in range(NT):
            for c in range(DCH):
                S.op("S", lambda e, c=c, t=t: e.dma_start(
                    out=x[c][:, :], in_=xin[c * 128:(c + 1) * 128, t * 1024:(t + 1) * 1024]),
                    reads=([f"xstb{c // 4}"] if t > 0 else []), writes=[f"x{c}"], dma=True,
                    semkey=("X" if c < 4 else "X2"))
            S.op("V", lambda e: e.engine_nop(), writes=[f"x{c}" for c in range(4)])
            S.op("V", lambda e: e.engine_nop(), writes=[f"x{c}" for c in range(4, DCH)])
            base = t * TILE_SLOTS
            if sublayers >= 1:
                emit_ln(t)
                emit_ffn(t, base)
            if sublayers >= 2:
                emit_ln(t)
                if debug_taps and debug_attn and t == 0:
                    for c in range(DCH):
                        S.op("S", lambda e, c=c: e.dma_start(out=dbg_xn[c], in_=xn[c][:, :]),
                             reads=[f"xn{c}"], writes=[f"dxn{c}"], dma=True)
                emit_attn(t, base + FFN_SLOTS)
                if debug_taps and debug_attn and t == 0:
                    for c in range(HCH):
                        S.op("S", lambda e, c=c: e.dma_start(out=dbg_h[c], in_=h[c][:, :]),
                             reads=[f"h{c}"], writes=[f"dh{c}"], dma=True)
                    for c in range(DCH):
                        S.op("S", lambda e, c=c: e.dma_start(out=dbg_ao[c], in_=ao[c][:, :]),
                             reads=[f"ao{c}"], writes=[f"dao{c}"], dma=True)
            if sublayers >= 3:
                if debug_taps and t == 0:
                    for c in range(DCH):
                        S.op("S", lambda e, c=c: e.dma_start(out=dbg_x2[c], in_=x[c][:, :]),
                             reads=[f"x{c}"], writes=[f"dx2{c}"], dma=True)
                emit_ln(t)
                if debug_taps and t == 0:
                    for c in range(DCH):
                        S.op("S", lambda e, c=c: e.dma_start(out=dbg_xn3[c], in_=xn[c][:, :]),
                             reads=[f"xn{c}"], writes=[f"dxn3{c}"], dma=True)
                emit_ffn(t, base + FFN_SLOTS + ATTN_SLOTS, store_x=True)
                if debug_taps and t == 0:
                    for c in range(HCH):
                        S.op("S", lambda e, c=c: e.dma_start(out=dbg_h3[c], in_=h[c][:, :]),
                             reads=[f"h{c}"], writes=[f"dh3{c}"], dma=True)
            else:
                state["loaded"] = (t + 1) * TILE_SLOTS
            if sublayers < 3:
                for c in range(DCH):
                    S.op("G", lambda e, c=c, t=t: e.dma_start(
                        out=xout[c * 128:(c + 1) * 128, t * 1024:(t + 1) * 1024], in_=x[c][:, :]),
                        reads=[f"x{c}"], writes=[f"xo{c}"], dma=True)
                for pas in range(2):
                    S.op("G", lambda e: e.engine_nop(),
                         reads=[f"xo{pas * 4 + oc}" for oc in range(4)],
                         writes=[f"xstb{pas}"])

        S.emit(block, sems)
    return nc


def _host_inputs(L0_attn_x, L0_attn_Wqkv, L0_attn_Wout, L0_ffn_w_gate,
                 L0_ffn_w_up, L0_ffn_w_down):
    xfull = np.ascontiguousarray(np.asarray(L0_attn_x, dtype=np.float32)).reshape(B * L, D)
    wqkv = np.asarray(L0_attn_Wqkv, dtype=np.float32)
    wout = np.asarray(L0_attn_Wout, dtype=np.float32)
    wg = np.asarray(L0_ffn_w_gate)
    wu = np.asarray(L0_ffn_w_up)
    wd = np.asarray(L0_ffn_w_down)

    def lhsT_slots(w):
        return np.ascontiguousarray(
            w.reshape(8, 128, 8, 128).transpose(0, 3, 2, 1).reshape(8, 128, 1024))

    wg_s = np.ascontiguousarray(
        wg.reshape(HCH, 128, 8, 128).transpose(0, 3, 2, 1).reshape(HCH, 128, 1024))
    wu_s = np.ascontiguousarray(
        wu.reshape(HCH, 128, 8, 128).transpose(0, 3, 2, 1).reshape(HCH, 128, 1024))
    gu = np.stack([wg_s, wu_s], axis=1).reshape(2 * HCH, 128, 1024)
    wd6 = wd.reshape(2, 4, 128, 12, 2, 128).transpose(0, 3, 5, 4, 1, 2)
    wd_s = np.ascontiguousarray(wd6.reshape(24, 128, 1024))
    ffn_stream = np.concatenate([gu, wd_s], axis=0).astype(NBF)
    assert ffn_stream.shape == (FFN_SLOTS, 128, 1024)

    pj = np.concatenate([np.arange(0, 64, 2), np.arange(1, 64, 2)])
    perm_rows = (np.arange(1024) // 64) * 64 + pj[np.arange(1024) % 64]
    wq_p = wqkv[0:1024][perm_rows]
    wk_p = wqkv[1024:2048][perm_rows]
    wv = wqkv[2048:3072]
    vs = np.ascontiguousarray(wv.reshape(1024, 8, 128).transpose(1, 2, 0))
    attn_stream = np.concatenate(
        [lhsT_slots(wq_p), lhsT_slots(wk_p), vs, lhsT_slots(wout)],
        axis=0).astype(np.float32)
    assert attn_stream.shape == (ATTN_SLOTS, 128, 1024)

    pvec = np.arange(128)
    inv_freq = 1.0 / 10000.0 ** (np.arange(0, HDIM, 2, dtype=np.float64) / HDIM)
    sign = np.where((pvec % 64) < 32, -1.0, 1.0)
    mask = np.zeros((128, 512), dtype=np.float32)
    for qc in range(2):
        qloc = qc * 128 + pvec
        mask[:, qc * 256:(qc + 1) * 256] = np.where(
            np.arange(256)[None, :] <= qloc[:, None], 0.0, -1e30)
    identf = np.eye(128, dtype=np.float32).astype(NBF)
    permm = np.zeros((128, 128), dtype=np.float32)
    src = (pvec // 64) * 64 + ((pvec % 64) + 32) % 64
    permm[src, pvec] = 1.0
    permm = permm.astype(NBF)

    in_maps = []
    for core in range(NCORES):
        rs = core * TPC
        posn = (rs % L) + np.arange(TPC, dtype=np.float64)
        theta = posn[None, :] * inv_freq[pvec % 32][:, None]
        cos_t = np.cos(theta).astype(np.float32)
        sin_t = (np.sin(theta) * sign[:, None]).astype(np.float32)
        xcore = np.ascontiguousarray(xfull[rs:rs + TPC].T)
        in_maps.append({
            "xin": xcore, "ffn_w": ffn_stream, "attn_w": attn_stream,
            "cos_t": cos_t, "sin_t": sin_t, "mask_t": mask,
            "idn_t": identf, "perm_t": permm,
        })
    return in_maps


def kernel(L0_attn_x, L0_attn_Wqkv, L0_attn_Wout, L0_ffn_w_gate,
           L0_ffn_w_up, L0_ffn_w_down):
    in_maps = _host_inputs(L0_attn_x, L0_attn_Wqkv, L0_attn_Wout,
                           L0_ffn_w_gate, L0_ffn_w_up, L0_ffn_w_down)
    nc = build_program(sublayers=_SUBLAYERS, debug_taps=_DEBUG_TAPS, debug_attn=_DEBUG_ATTN)
    res = run_bass_kernel_spmd(nc, in_maps, core_ids=list(range(NCORES)))
    if _DEBUG_TAPS:
        global _DEBUG_RES
        _DEBUG_RES = res.results
    shards = [np.asarray(r["xout"]).T for r in res.results]
    return np.concatenate(shards, axis=0).reshape(B, L, D)


# version 10 (final)
# speedup vs baseline: 1.4932x; 1.0087x over previous
from contextlib import ExitStack

import numpy as np
import ml_dtypes
import concourse.bass as bass
import concourse.mybir as mybir
from concourse.bass_utils import run_bass_kernel_spmd

F32 = mybir.dt.float32
BF16 = mybir.dt.bfloat16
AF = mybir.ActivationFunctionType
ALU = mybir.AluOpType
AX = mybir.AxisListType
NBF = ml_dtypes.bfloat16

B, L, D = 2, 16384, 1024
HID, NH, HDIM, BS = 3072, 16, 64, 256
NCORES = 8
TPC = (B * L) // NCORES
TILE = 1024
NT = TPC // TILE
DCH = D // 128
HCH = HID // 128
SD = 0.5 / (127.0 * 127.0)
NBH = 4 * NH

_SUBLAYERS = 3

FFN_SLOTS = 72
ATTN_SLOTS = 32
TILE_SLOTS = FFN_SLOTS + ATTN_SLOTS + FFN_SLOTS


class Op:
    __slots__ = ("eng", "fn", "dma", "deps", "needs_inc", "cum", "semkey",
                 "hazard")

    def __init__(self, eng, fn, dma):
        self.eng, self.fn, self.dma = eng, fn, dma
        self.deps = []
        self.needs_inc = False
        self.cum = None
        self.hazard = False
        self.semkey = "D" if (eng == "G" and dma) else eng


class Sched:

    def __init__(self):
        self.ops = {e: [] for e in "SPVAG"}
        self.bufs = {}

    def _buf(self, name):
        b = self.bufs.get(name)
        if b is None:
            b = {"w": None, "r": []}
            self.bufs[name] = b
        return b

    def op(self, eng, fn, reads=(), writes=(), dma=False, semkey=None,
           hazard=False):
        o = Op(eng, fn, dma)
        o.hazard = hazard
        if semkey is not None:
            o.semkey = semkey
        deps = set()
        for n in reads:
            b = self._buf(n)
            if b["w"] is not None:
                deps.add(b["w"])
        for n in writes:
            b = self._buf(n)
            if b["w"] is not None:
                deps.add(b["w"])
            deps.update(b["r"])
        for n in reads:
            self._buf(n)["r"].append(o)
        for n in writes:
            b = self._buf(n)
            b["w"] = o
            b["r"] = []
        for d in deps:
            if d is o:
                continue
            if d.eng == eng and not d.dma and not d.hazard:
                continue
            o.deps.append(d)
            d.needs_inc = True
        self.ops[eng].append(o)
        return o

    SEMKEYS = tuple("SPVAGD") + ("K0", "K1", "K2", "C0", "C1", "X", "X2",
                                 "R0", "R1", "R2", "R3",
                                 "T0", "T1", "T2", "T3")

    def emit(self, block, sems):
        totals = {k: 0 for k in self.SEMKEYS}
        for e, lst in self.ops.items():
            for o in lst:
                if o.needs_inc or o.dma:
                    o.needs_inc = True
                    totals[o.semkey] += 16 if o.dma else 1
                o.cum = totals[o.semkey]

        def body_for(e):
            def body(eng):
                hw = {k: 0 for k in totals}
                for o in self.ops[e]:
                    need = {}
                    for d in o.deps:
                        if d.cum > need.get(d.semkey, 0):
                            need[d.semkey] = d.cum
                    for pe, v in sorted(need.items()):
                        if v > hw[pe]:
                            eng.wait_ge(sems[pe], v)
                            hw[pe] = v
                    inst = o.fn(eng)
                    if o.needs_inc:
                        inst.then_inc(sems[o.semkey], 16 if o.dma else 1)
                if e == "S":
                    for k in ("S", "K0", "K1", "K2", "C0", "C1", "X", "X2",
                              "R0", "R1", "R2", "R3"):
                        if totals[k] > 0:
                            eng.wait_ge(sems[k], totals[k])
                if e == "A":
                    for k in ("T0", "T1", "T2", "T3"):
                        if totals[k] > 0:
                            eng.wait_ge(sems[k], totals[k])
                if e == "G" and totals["D"] > 0:
                    eng.wait_ge(sems["D"], totals["D"])
            return body

        block.sync(body_for("S"))
        block.tensor(body_for("P"))
        block.vector(body_for("V"))
        block.scalar(body_for("A"))
        block.gpsimd(body_for("G"))


_DEBUG_TAPS = False
_DEBUG_ATTN = True


def build_program(sublayers=3, debug_taps=False, debug_attn=True):
    nc = bass.Bass("TRN2", target_bir_lowering=False)

    xin = nc.dram_tensor("xin", [D, TPC], F32, kind="ExternalInput").ap()
    if debug_taps:
        dbg_h = nc.dram_tensor("dbg_h", [HCH, 128, 1024], BF16, kind="ExternalOutput").ap()
        dbg_ao = nc.dram_tensor("dbg_ao", [DCH, 128, 1024], BF16, kind="ExternalOutput").ap()
        dbg_xn = nc.dram_tensor("dbg_xn", [DCH, 128, 1024], BF16, kind="ExternalOutput").ap()
        dbg_x2 = nc.dram_tensor("dbg_x2", [DCH, 128, 1024], F32, kind="ExternalOutput").ap()
        dbg_xn3 = nc.dram_tensor("dbg_xn3", [DCH, 128, 1024], BF16, kind="ExternalOutput").ap()
        dbg_h3 = nc.dram_tensor("dbg_h3", [HCH, 128, 1024], BF16, kind="ExternalOutput").ap()
    xout = nc.dram_tensor("xout", [D, TPC], F32, kind="ExternalOutput").ap()
    ffn_w = nc.dram_tensor("ffn_w", [FFN_SLOTS, 128, 1024], BF16, kind="ExternalInput").ap()
    attn_w = nc.dram_tensor("attn_w", [ATTN_SLOTS, 128, 1024], F32, kind="ExternalInput").ap()
    cos_t = nc.dram_tensor("cos_t", [128, TPC], F32, kind="ExternalInput").ap()
    sin_t = nc.dram_tensor("sin_t", [128, TPC], F32, kind="ExternalInput").ap()
    mask_t = nc.dram_tensor("mask_t", [128, 512], F32, kind="ExternalInput").ap()
    idn_t = nc.dram_tensor("idn_t", [128, 128], BF16, kind="ExternalInput").ap()
    perm_t = nc.dram_tensor("perm_t", [128, 128], BF16, kind="ExternalInput").ap()
    attn_wb = nc.dram_tensor("attn_wb", [ATTN_SLOTS, 128, 1024], BF16, kind="Internal").ap()

    with ExitStack() as ctx:
        def sb(name, shape, dt):
            return ctx.enter_context(nc.sbuf_tensor(name, shape, dt))

        x = [sb(f"x{c}", [128, 1024], F32) for c in range(DCH)]
        xn = [sb(f"xn{c}", [128, 1024], BF16) for c in range(DCH)]
        h = [sb(f"h{c}", [128, 1024], BF16) for c in range(HCH)]
        ao = [sb(f"ao{c}", [128, 1024], BF16) for c in range(DCH)]
        ring = sb("ring", [128, 8 * 1024], BF16)
        cosb = sb("cosb", [128, 1024], F32)
        sinb = sb("sinb", [128, 1024], F32)
        ts_ = [sb(f"t{i}", [128, 1024], F32) for i in range(4)]
        bs = [sb(f"bs{i}", [128, 1024], BF16) for i in range(2)]
        sgb = [sb(f"sgb{i}", [128, 512], BF16) for i in range(2)]
        smv = [sb(f"smv{i}", [128, 512], F32) for i in range(4)]
        ae = [sb(f"ae{i}", [128, 512], BF16) for i in range(4)]
        amn = [sb(f"amn{i}", [128, 512], BF16) for i in range(4)]
        atT = [sb(f"atT{i}", [128, 512], BF16) for i in range(4)]
        negm = [sb(f"negm{i}", [128, 2], F32) for i in range(4)]
        den4 = [sb(f"den4{i}", [128, 4], F32) for i in range(4)]
        rden4 = [sb(f"rden4{i}", [128, 4], F32) for i in range(4)]
        rowsA = sb("rowsA", [1, 1024], F32)
        rowsB = sb("rowsB", [1, 1024], F32)
        rowsC = sb("rowsC", [1, 1024], F32)
        maskb = sb("maskb", [128, 512], F32)
        identb = sb("identb", [128, 128], BF16)
        permb = sb("permb", [128, 128], BF16)
        onesc = sb("onesc", [128, 1], BF16)
        onesr = sb("onesr", [1, 128], F32)
        epsb = sb("epsb", [128, 1], F32)
        pb = [ctx.enter_context(nc.psum_tensor(f"pb{i}", [128, 512], F32)) for i in range(8)]

        sems = {e: ctx.enter_context(nc.semaphore(f"sem{e}")) for e in Sched.SEMKEYS}
        block = ctx.enter_context(nc.Block())

        S = Sched()

        def rname(i):
            return f"ring{i % 8}"

        def rslot(i):
            j = i % 8
            return ring[:, j * 1024:(j + 1) * 1024]

        S.op("S", lambda e: e.dma_start(out=maskb[:, :], in_=mask_t), writes=["maskb"],
             dma=True, semkey="K0")
        S.op("S", lambda e: e.dma_start(out=identb[:, :], in_=idn_t), writes=["identb"],
             dma=True, semkey="K1")
        S.op("S", lambda e: e.dma_start(out=permb[:, :], in_=perm_t), writes=["permb"],
             dma=True, semkey="K2")
        S.op("V", lambda e: e.memset(onesc[:, :], 1.0), writes=["onesc"])
        S.op("V", lambda e: e.memset(onesr[:, :], 1.0), writes=["onesr"])
        S.op("V", lambda e: e.memset(epsb[:, :], 1e-5), writes=["epsb"])
        for i in range(4):
            S.op("V", lambda e, i=i: e.memset(den4[i][:, :], 1.0), writes=[f"den4{i}"])
        for j in range(0, ATTN_SLOTS, 4):
            S.op("G", lambda e, j=j: e.dma_start(out=attn_wb[j:j + 4], in_=attn_w[j:j + 4]),
                 writes=[f"asb{k}" for k in range(j, j + 4)], dma=True)
        S.op("G", lambda e: e.engine_nop(),
             reads=[f"asb{k}" for k in range(ATTN_SLOTS)], writes=["asball"])

        state = {"loaded": 0}

        def src_of(slot):
            s = slot % TILE_SLOTS
            if s < FFN_SLOTS:
                return ffn_w, s, []
            if s < FFN_SLOTS + ATTN_SLOTS:
                j = s - FFN_SLOTS
                return attn_wb, j, ["asball"]
            return ffn_w, s - FFN_SLOTS - ATTN_SLOTS, []

        def need_slot(gi):
            while state["loaded"] <= gi:
                i = state["loaded"]
                arr, j, rd = src_of(i)
                arr2, j2, rd2 = src_of(i + 1)
                assert arr is arr2 and j2 == j + 1 and i % 2 == 0
                pos = i % 8
                out_ap = ring[:, pos * 1024:(pos + 2) * 1024].rearrange(
                    "p (s j) -> p s j", s=2)
                in_ap = arr[j:j + 2].rearrange("s p j -> p s j")
                S.op("S", lambda e, o=out_ap, i_=in_ap: e.dma_start(out=o, in_=i_),
                     reads=rd + rd2, writes=[rname(i), rname(i + 1)], dma=True,
                     semkey=f"R{(i % 8) // 2}")
                state["loaded"] += 2

        def emit_ln(t):
            for c in range(DCH):
                S.op("A", lambda e, c=c: e.copy(out=xn[c][:, :], in_=x[c][:, :]),
                     reads=[f"x{c}"], writes=[f"xn{c}"])
                p = c % 2
                S.op("A", lambda e, c=c, p=p: e.activation(
                    out=bs[p][:, :], in_=xn[c][:, :], func=AF.Square),
                    reads=[f"xn{c}"], writes=[f"bs{p}"])
                for hf in range(2):
                    S.op("P", lambda e, c=c, hf=hf: e.matmul(
                        pb[4 + hf][0:1, 0:512], onesc[:, :],
                        xn[c][:, hf * 512:(hf + 1) * 512],
                        start=(c == 0), stop=(c == DCH - 1)),
                        reads=[f"xn{c}", "onesc"], writes=[f"pb{4 + hf}"])
                    S.op("P", lambda e, c=c, p=p, hf=hf: e.matmul(
                        pb[6 + hf][0:1, 0:512], onesc[:, :],
                        bs[p][:, hf * 512:(hf + 1) * 512],
                        start=(c == 0), stop=(c == DCH - 1)),
                        reads=[f"bs{p}", "onesc"], writes=[f"pb{6 + hf}"])
            for hf in range(2):
                S.op("V", lambda e, hf=hf: e.tensor_scalar(
                    out=rowsA[0:1, hf * 512:(hf + 1) * 512], in0=pb[4 + hf][0:1, 0:512],
                    scalar1=-1.0 / D, scalar2=None, op0=ALU.mult),
                    reads=[f"pb{4 + hf}"], writes=["rowsA"])
            for hf in range(2):
                S.op("P", lambda e, hf=hf: e.matmul(
                    pb[2 + hf][:, 0:512], onesr[0:1, :], rowsA[0:1, hf * 512:(hf + 1) * 512],
                    start=True, stop=True),
                    reads=["rowsA", "onesr"], writes=[f"pb{2 + hf}"])
                S.op("A", lambda e, hf=hf: e.copy(
                    out=ts_[1][:, hf * 512:(hf + 1) * 512], in_=pb[2 + hf][:, 0:512]),
                    reads=[f"pb{2 + hf}"], writes=["t1"])
            for hf in range(2):
                S.op("A", lambda e, hf=hf: e.activation(
                    out=rowsC[0:1, hf * 512:(hf + 1) * 512], in_=pb[4 + hf][0:1, 0:512],
                    func=AF.Square),
                    reads=[f"pb{4 + hf}"], writes=["rowsC"])
                S.op("V", lambda e, hf=hf: e.scalar_tensor_tensor(
                    out=rowsB[0:1, hf * 512:(hf + 1) * 512],
                    in0=rowsC[0:1, hf * 512:(hf + 1) * 512], scalar=-1.0 / D,
                    in1=pb[6 + hf][0:1, 0:512], op0=ALU.mult, op1=ALU.add),
                    reads=["rowsC", f"pb{6 + hf}"], writes=["rowsB"])
            S.op("A", lambda e: e.activation(
                out=rowsB[0:1, 0:1024], in_=rowsB[0:1, 0:1024], func=AF.Ln,
                scale=1.0 / D, bias=epsb[0:1, 0:1]),
                reads=["rowsB", "epsb"], writes=["rowsB"])
            S.op("A", lambda e: e.activation(
                out=rowsC[0:1, 0:1024], in_=rowsB[0:1, 0:1024], func=AF.Exp,
                scale=-0.5),
                reads=["rowsB"], writes=["rowsC"])
            for hf in range(2):
                S.op("P", lambda e, hf=hf: e.matmul(
                    pb[hf][:, 0:512], onesr[0:1, :], rowsC[0:1, hf * 512:(hf + 1) * 512],
                    start=True, stop=True),
                    reads=["rowsC", "onesr"], writes=[f"pb{hf}"])
                S.op("A", lambda e, hf=hf: e.copy(
                    out=ts_[0][:, hf * 512:(hf + 1) * 512], in_=pb[hf][:, 0:512]),
                    reads=[f"pb{hf}"], writes=["t0"])
            for c in range(DCH):
                tp = 2 + (c % 2)
                S.op("G", lambda e, c=c, tp=tp: e.tensor_add(
                    ts_[tp][:, :], x[c][:, :], ts_[1][:, :]),
                    reads=[f"x{c}", "t1"], writes=[f"t{tp}"])
                S.op("V", lambda e, c=c, tp=tp: e.tensor_mul(
                    xn[c][:, :], ts_[tp][:, :], ts_[0][:, :]),
                    reads=[f"t{tp}", "t0"], writes=[f"xn{c}"])

        def emit_ffn(t, base_slot, store_x=False):
            def gu_F0(p):
                hc, th = divmod(p, 2)
                gslot, uslot = base_slot + 2 * hc, base_slot + 2 * hc + 1
                need_slot(uslot)
                gb_, ub_ = (0, 4) if hc % 2 == 0 else (2, 6)
                for kc in range(DCH):
                    S.op("P", lambda e, gslot=gslot, gb_=gb_, th=th, kc=kc: e.matmul(
                        pb[gb_ + th][:, 0:512], rslot(gslot)[:, kc * 128:(kc + 1) * 128],
                        xn[kc][:, th * 512:(th + 1) * 512],
                        start=(kc == 0), stop=(kc == DCH - 1)),
                        reads=[rname(gslot), f"xn{kc}"], writes=[f"pb{gb_ + th}"])
                for kc in range(DCH):
                    S.op("P", lambda e, uslot=uslot, ub_=ub_, th=th, kc=kc: e.matmul(
                        pb[ub_ + th][:, 0:512], rslot(uslot)[:, kc * 128:(kc + 1) * 128],
                        xn[kc][:, th * 512:(th + 1) * 512],
                        start=(kc == 0), stop=(kc == DCH - 1)),
                        reads=[rname(uslot), f"xn{kc}"], writes=[f"pb{ub_ + th}"])

            def gu_F1(p):
                hc, th = divmod(p, 2)
                gb_ = 0 if hc % 2 == 0 else 2
                S.op("A", lambda e, gb_=gb_, th=th: e.activation(
                    out=sgb[th][:, :], in_=pb[gb_ + th][:, 0:512], func=AF.Silu,
                    scale=1.0 / 127.0),
                    reads=[f"pb{gb_ + th}"], writes=[f"sgb{th}"])

            def gu_F2(p):
                hc, th = divmod(p, 2)
                ub_ = 4 if hc % 2 == 0 else 6
                S.op("V", lambda e, hc=hc, ub_=ub_, th=th: e.tensor_mul(
                    h[hc][:, th * 512:(th + 1) * 512], sgb[th][:, :],
                    pb[ub_ + th][:, 0:512]),
                    reads=[f"sgb{th}", f"pb{ub_ + th}"], writes=[f"h{hc}"])

            gu_stages = [gu_F0, gu_F1, gu_F2]
            for tt in range(2 * HCH + 2):
                for k, st in enumerate(gu_stages):
                    p = tt - k
                    if 0 <= p < 2 * HCH:
                        st(p)
            for pas in range(2):
                for hc in range(HCH):
                    slot = base_slot + 48 + pas * 12 + hc // 2
                    need_slot(slot)
                    for oc in range(4):
                        off = (((hc % 2) * 4) + oc) * 128
                        for th in range(2):
                            S.op("P", lambda e, slot=slot, off=off, oc=oc, th=th, hc=hc: e.matmul(
                                pb[oc * 2 + th][:, 0:512],
                                rslot(slot)[:, off:off + 128],
                                h[hc][:, th * 512:(th + 1) * 512],
                                start=(hc == 0), stop=(hc == HCH - 1)),
                                reads=[rname(slot), f"h{hc}"], writes=[f"pb{oc * 2 + th}"])
                for oc in range(4):
                    xc = pas * 4 + oc
                    for th in range(2):
                        S.op("V", lambda e, oc=oc, th=th, xc=xc: e.scalar_tensor_tensor(
                            out=x[xc][:, th * 512:(th + 1) * 512],
                            in0=pb[oc * 2 + th][:, 0:512], scalar=SD,
                            in1=x[xc][:, th * 512:(th + 1) * 512],
                            op0=ALU.mult, op1=ALU.add),
                            reads=[f"pb{oc * 2 + th}", f"x{xc}"], writes=[f"x{xc}"])
                if store_x:
                    for oc in range(4):
                        xc = pas * 4 + oc
                        S.op("G", lambda e, xc=xc, t=t: e.dma_start(
                            out=xout[xc * 128:(xc + 1) * 128, t * 1024:(t + 1) * 1024],
                            in_=x[xc][:, :]),
                            reads=[f"x{xc}"], writes=[f"xo{xc}"], dma=True)
                    S.op("G", lambda e: e.engine_nop(),
                         reads=[f"xo{pas * 4 + oc}" for oc in range(4)],
                         writes=[f"xstb{pas}"])

        def emit_attn(t, base_slot):
            tc0 = t * 1024
            S.op("S", lambda e, t=t: e.dma_start(
                out=cosb[:, :], in_=cos_t[:, t * 1024:(t + 1) * 1024]),
                writes=["cosb"], dma=True, semkey="C0")
            S.op("S", lambda e, t=t: e.dma_start(
                out=sinb[:, :], in_=sin_t[:, t * 1024:(t + 1) * 1024]),
                writes=["sinb"], dma=True, semkey="C1")
            def qk_Q0(j):
                qk, oc = divmod(j, DCH)
                slot = base_slot + qk * 8 + oc
                need_slot(slot)
                qb_ = (0, 1) if j % 2 == 0 else (2, 3)
                for th in range(2):
                    for kc in range(DCH):
                        S.op("P", lambda e, slot=slot, qb_=qb_, th=th, kc=kc: e.matmul(
                            pb[qb_[th]][:, 0:512], rslot(slot)[:, kc * 128:(kc + 1) * 128],
                            xn[kc][:, th * 512:(th + 1) * 512],
                            start=(kc == 0), stop=(kc == DCH - 1)),
                            reads=[rname(slot), f"xn{kc}"], writes=[f"pb{qb_[th]}"])

            def qk_Q1(j):
                qk = j // DCH
                par = j % 2
                qb_ = (0, 1) if par == 0 else (2, 3)
                sw_ = (4, 5) if par == 0 else (6, 7)
                for th in range(2):
                    S.op("A", lambda e, qb_=qb_, th=th, par=par, qk=qk: e.activation(
                        out=bs[par][:, th * 512:(th + 1) * 512], in_=pb[qb_[th]][:, 0:512],
                        func=AF.Copy, scale=(0.125 if qk == 0 else 1.0)),
                        reads=[f"pb{qb_[th]}"], writes=[f"bs{par}"])
                    S.op("P", lambda e, sw_=sw_, th=th, par=par: e.matmul(
                        pb[sw_[th]][:, 0:512], permb[:, :],
                        bs[par][:, th * 512:(th + 1) * 512], start=True, stop=True),
                        reads=["permb", f"bs{par}"], writes=[f"pb{sw_[th]}"])

            def qk_Q2(j):
                par = j % 2
                sw_ = (4, 5) if par == 0 else (6, 7)
                tA, tB = 2 + par, 0 + par
                S.op("V", lambda e, par=par, tA=tA: e.tensor_mul(
                    ts_[tA][:, :], bs[par][:, :], cosb[:, 0:1024]),
                    reads=[f"bs{par}", "cosb"], writes=[f"t{tA}"])
                for th in range(2):
                    S.op("V", lambda e, sw_=sw_, th=th, tB=tB: e.tensor_mul(
                        ts_[tB][:, th * 512:(th + 1) * 512], pb[sw_[th]][:, 0:512],
                        sinb[:, th * 512:(th + 1) * 512]),
                        reads=[f"pb{sw_[th]}", "sinb"], writes=[f"t{tB}"])

            def qk_Q3(j):
                par = j % 2
                tA, tB = 2 + par, 0 + par
                S.op("G", lambda e, hdst=j, tA=tA, tB=tB: e.tensor_add(
                    h[hdst][:, :], ts_[tA][:, :], ts_[tB][:, :]),
                    reads=[f"t{tA}", f"t{tB}"], writes=[f"h{j}"])

            qk_stages = [qk_Q0, qk_Q1, qk_Q2, qk_Q3]
            for tt in range(16 + 3):
                for k, st in enumerate(qk_stages):
                    j = tt - k
                    if 0 <= j < 16:
                        st(j)
            need_slot(base_slot + 23)

            def vp_W0(tc):
                vb_ = (0, 1) if tc % 2 == 0 else (2, 3)
                for vh in range(2):
                    for kc in range(DCH):
                        slot = base_slot + 16 + kc
                        S.op("P", lambda e, slot=slot, vb_=vb_, vh=vh, kc=kc, tc=tc: e.matmul(
                            pb[vb_[vh]][:, 0:512], xn[kc][:, tc * 128:(tc + 1) * 128],
                            rslot(slot)[:, vh * 512:(vh + 1) * 512],
                            start=(kc == 0), stop=(kc == DCH - 1)),
                            reads=[rname(slot), f"xn{kc}"], writes=[f"pb{vb_[vh]}"])

            def vp_W1(tc):
                vb_ = (0, 1) if tc % 2 == 0 else (2, 3)
                for vh in range(2):
                    S.op("A", lambda e, tc=tc, vb_=vb_, vh=vh: e.copy(
                        out=h[16 + tc][:, vh * 512:(vh + 1) * 512], in_=pb[vb_[vh]][:, 0:512]),
                        reads=[f"pb{vb_[vh]}"], writes=[f"h{16 + tc}"])

            for tt in range(8 + 1):
                if tt < 8:
                    vp_W0(tt)
                if tt >= 1:
                    vp_W1(tt - 1)
            def bh_T0(i):
                cp, r = divmod(i, 8)
                hh, b4 = divmod(r, 4)
                qbase, sb_ = hh * 64, i % 3
                for qc in range(2):
                    S.op("P", lambda e, sb_=sb_, qc=qc, cp=cp, qbase=qbase, b4=b4: e.matmul(
                        pb[sb_][:, qc * 256:(qc + 1) * 256],
                        h[cp][qbase:qbase + 64, b4 * 256 + qc * 128:b4 * 256 + (qc + 1) * 128],
                        h[8 + cp][qbase:qbase + 64, b4 * 256:(b4 + 1) * 256],
                        start=True, stop=True),
                        reads=[f"h{cp}", f"h{8 + cp}"], writes=[f"pb{sb_}"])

            def bh_T1(i):
                sb_, p2, p4 = i % 3, i % 4, i % 4
                S.op("V", lambda e, sb_=sb_, p2=p2: e.tensor_add(
                    smv[p2][:, 0:512], pb[sb_][:, 0:512], maskb[:, :]),
                    reads=[f"pb{sb_}", "maskb"], writes=[f"smv{p2}"])
                S.op("V", lambda e, p2=p2, p4=p4: e.reduce_max(
                    negm[p4][:, 0:2],
                    smv[p2][:, 0:512].rearrange("p (c n) -> p c n", c=2),
                    axis=AX.X, negate=True),
                    reads=[f"smv{p2}"], writes=[f"negm{p4}"], hazard=True)

            def bh_T2(i):
                p2, p3, p4 = i % 4, i % 4, i % 4
                for qc in range(2):
                    S.op("A", lambda e, p2=p2, p3=p3, p4=p4, qc=qc: e.activation(
                        out=ae[p3][:, qc * 256:(qc + 1) * 256],
                        in_=smv[p2][:, qc * 256:(qc + 1) * 256], func=AF.Exp,
                        bias=negm[p4][:, qc:qc + 1], scale=1.0,
                        accum_out=den4[p4][:, qc:qc + 1]),
                        reads=[f"smv{p2}", f"negm{p4}"],
                        writes=[f"ae{p3}", f"den4{p4}"], hazard=True)

            def bh_T3(i):
                p3, p4 = i % 4, i % 4
                S.op("V", lambda e, p4=p4: e.reciprocal(
                    rden4[p4][:, 0:4], den4[p4][:, 0:4]),
                    reads=[f"den4{p4}"], writes=[f"rden4{p4}"], hazard=True)
                for qc in range(2):
                    S.op("V", lambda e, p3=p3, p4=p4, qc=qc: e.tensor_scalar(
                        out=amn[p3][:, qc * 256:(qc + 1) * 256],
                        in0=ae[p3][:, qc * 256:(qc + 1) * 256],
                        scalar1=rden4[p4][:, qc:qc + 1], scalar2=None,
                        op0=ALU.mult),
                        reads=[f"ae{p3}", f"rden4{p4}"], writes=[f"amn{p3}"])

            def bh_T4(i):
                tb_, p3 = 3 + (i % 2), i % 4
                tbf = pb[tb_][:, 0:256].bitcast(BF16)
                for kk in range(2):
                    for qc in range(2):
                        S.op("P", lambda e, tbf=tbf, kk=kk, qc=qc, p3=p3: e.transpose(
                            tbf[:, kk * 256 + qc * 128:kk * 256 + (qc + 1) * 128],
                            amn[p3][:, qc * 256 + kk * 128:qc * 256 + (kk + 1) * 128],
                            identb[:, :]),
                            reads=[f"amn{p3}", "identb"], writes=[f"pb{tb_}"])

            def bh_T5(i):
                tb_, p3 = 3 + (i % 2), i % 4
                tbf = pb[tb_][:, 0:256].bitcast(BF16)
                S.op("A", lambda e, tbf=tbf, p3=p3: e.copy(
                    out=atT[p3][:, 0:256], in_=tbf[:, 0:256]),
                    reads=[f"pb{tb_}"], writes=[f"atT{p3}"])
                S.op("V", lambda e, tbf=tbf, p3=p3: e.tensor_copy(
                    out=atT[p3][:, 256:512], in_=tbf[:, 256:512]),
                    reads=[f"pb{tb_}"], writes=[f"atT{p3}"])

            def bh_T6(i):
                cp, r = divmod(i, 8)
                hh, b4 = divmod(r, 4)
                hd, qbase = 2 * cp + hh, hh * 64
                ab_, p3 = 5 + (i % 2), i % 4
                for kk in range(2):
                    S.op("P", lambda e, ab_=ab_, b4=b4, kk=kk, hd=hd, p3=p3, qbase=qbase: e.matmul(
                        pb[ab_][qbase:qbase + 64, 0:256],
                        h[16 + b4 * 2 + kk][:, hd * 64:(hd + 1) * 64],
                        atT[p3][:, kk * 256:(kk + 1) * 256],
                        start=(kk == 0), stop=(kk == 1),
                        tile_position=(0, qbase)),
                        reads=[f"h{16 + b4 * 2 + kk}", f"atT{p3}"], writes=[f"pb{ab_}"])
                S.op("A", lambda e, ab_=ab_, cp=cp, qbase=qbase, b4=b4: e.copy(
                    out=ao[cp][qbase:qbase + 64, b4 * 256:(b4 + 1) * 256],
                    in_=pb[ab_][qbase:qbase + 64, 0:256]),
                    reads=[f"pb{ab_}"], writes=[f"ao{cp}"])

            bh_stages = [bh_T0, bh_T1, bh_T2, bh_T3, bh_T4, bh_T5, bh_T6]
            for tt in range(NBH + len(bh_stages) - 1):
                for k, st in enumerate(bh_stages):
                    i = tt - k
                    if 0 <= i < NBH:
                        st(i)
            for oc in range(DCH):
                slot = base_slot + 24 + oc
                need_slot(slot)
                ob_ = 4 + oc % 4
                for th in range(2):
                    for kc in range(DCH):
                        S.op("P", lambda e, slot=slot, ob_=ob_, th=th, kc=kc: e.matmul(
                            pb[ob_][:, 0:512], rslot(slot)[:, kc * 128:(kc + 1) * 128],
                            ao[kc][:, th * 512:(th + 1) * 512],
                            start=(kc == 0), stop=(kc == DCH - 1)),
                            reads=[rname(slot), f"ao{kc}"], writes=[f"pb{ob_}"])
                    S.op("V", lambda e, oc=oc, th=th, ob_=ob_: e.scalar_tensor_tensor(
                        out=x[oc][:, th * 512:(th + 1) * 512], in0=pb[ob_][:, 0:512],
                        scalar=1.0, in1=x[oc][:, th * 512:(th + 1) * 512],
                        op0=ALU.mult, op1=ALU.add),
                        reads=[f"pb{ob_}", f"x{oc}"], writes=[f"x{oc}"])

        for t in range(NT):
            for c in range(DCH):
                S.op("S", lambda e, c=c, t=t: e.dma_start(
                    out=x[c][:, :], in_=xin[c * 128:(c + 1) * 128, t * 1024:(t + 1) * 1024]),
                    reads=([f"xstb{c // 4}"] if t > 0 else []), writes=[f"x{c}"], dma=True,
                    semkey=("X" if c < 4 else "X2"))
            S.op("V", lambda e: e.engine_nop(), writes=[f"x{c}" for c in range(4)])
            S.op("V", lambda e: e.engine_nop(), writes=[f"x{c}" for c in range(4, DCH)])
            base = t * TILE_SLOTS
            if sublayers >= 1:
                emit_ln(t)
                emit_ffn(t, base)
            if sublayers >= 2:
                emit_ln(t)
                if debug_taps and debug_attn and t == 0:
                    for c in range(DCH):
                        S.op("S", lambda e, c=c: e.dma_start(out=dbg_xn[c], in_=xn[c][:, :]),
                             reads=[f"xn{c}"], writes=[f"dxn{c}"], dma=True)
                emit_attn(t, base + FFN_SLOTS)
                if debug_taps and debug_attn and t == 0:
                    for c in range(HCH):
                        S.op("S", lambda e, c=c: e.dma_start(out=dbg_h[c], in_=h[c][:, :]),
                             reads=[f"h{c}"], writes=[f"dh{c}"], dma=True)
                    for c in range(DCH):
                        S.op("S", lambda e, c=c: e.dma_start(out=dbg_ao[c], in_=ao[c][:, :]),
                             reads=[f"ao{c}"], writes=[f"dao{c}"], dma=True)
            if sublayers >= 3:
                if debug_taps and t == 0:
                    for c in range(DCH):
                        S.op("S", lambda e, c=c: e.dma_start(out=dbg_x2[c], in_=x[c][:, :]),
                             reads=[f"x{c}"], writes=[f"dx2{c}"], dma=True)
                emit_ln(t)
                if debug_taps and t == 0:
                    for c in range(DCH):
                        S.op("S", lambda e, c=c: e.dma_start(out=dbg_xn3[c], in_=xn[c][:, :]),
                             reads=[f"xn{c}"], writes=[f"dxn3{c}"], dma=True)
                emit_ffn(t, base + FFN_SLOTS + ATTN_SLOTS, store_x=True)
                if debug_taps and t == 0:
                    for c in range(HCH):
                        S.op("S", lambda e, c=c: e.dma_start(out=dbg_h3[c], in_=h[c][:, :]),
                             reads=[f"h{c}"], writes=[f"dh3{c}"], dma=True)
            else:
                state["loaded"] = (t + 1) * TILE_SLOTS
            if sublayers < 3:
                for c in range(DCH):
                    S.op("G", lambda e, c=c, t=t: e.dma_start(
                        out=xout[c * 128:(c + 1) * 128, t * 1024:(t + 1) * 1024], in_=x[c][:, :]),
                        reads=[f"x{c}"], writes=[f"xo{c}"], dma=True)
                for pas in range(2):
                    S.op("G", lambda e: e.engine_nop(),
                         reads=[f"xo{pas * 4 + oc}" for oc in range(4)],
                         writes=[f"xstb{pas}"])

        S.emit(block, sems)
    return nc


def _host_inputs(L0_attn_x, L0_attn_Wqkv, L0_attn_Wout, L0_ffn_w_gate,
                 L0_ffn_w_up, L0_ffn_w_down):
    xfull = np.ascontiguousarray(np.asarray(L0_attn_x, dtype=np.float32)).reshape(B * L, D)
    wqkv = np.asarray(L0_attn_Wqkv, dtype=np.float32)
    wout = np.asarray(L0_attn_Wout, dtype=np.float32)
    wg = np.asarray(L0_ffn_w_gate)
    wu = np.asarray(L0_ffn_w_up)
    wd = np.asarray(L0_ffn_w_down)

    def lhsT_slots(w):
        return np.ascontiguousarray(
            w.reshape(8, 128, 8, 128).transpose(0, 3, 2, 1).reshape(8, 128, 1024))

    wg_s = np.ascontiguousarray(
        wg.reshape(HCH, 128, 8, 128).transpose(0, 3, 2, 1).reshape(HCH, 128, 1024))
    wu_s = np.ascontiguousarray(
        wu.reshape(HCH, 128, 8, 128).transpose(0, 3, 2, 1).reshape(HCH, 128, 1024))
    gu = np.stack([wg_s, wu_s], axis=1).reshape(2 * HCH, 128, 1024)
    wd6 = wd.reshape(2, 4, 128, 12, 2, 128).transpose(0, 3, 5, 4, 1, 2)
    wd_s = np.ascontiguousarray(wd6.reshape(24, 128, 1024))
    ffn_stream = np.concatenate([gu, wd_s], axis=0).astype(NBF)
    assert ffn_stream.shape == (FFN_SLOTS, 128, 1024)

    pj = np.concatenate([np.arange(0, 64, 2), np.arange(1, 64, 2)])
    perm_rows = (np.arange(1024) // 64) * 64 + pj[np.arange(1024) % 64]
    wq_p = wqkv[0:1024][perm_rows]
    wk_p = wqkv[1024:2048][perm_rows]
    wv = wqkv[2048:3072]
    vs = np.ascontiguousarray(wv.reshape(1024, 8, 128).transpose(1, 2, 0))
    attn_stream = np.concatenate(
        [lhsT_slots(wq_p), lhsT_slots(wk_p), vs, lhsT_slots(wout)],
        axis=0).astype(np.float32)
    assert attn_stream.shape == (ATTN_SLOTS, 128, 1024)

    pvec = np.arange(128)
    inv_freq = 1.0 / 10000.0 ** (np.arange(0, HDIM, 2, dtype=np.float64) / HDIM)
    sign = np.where((pvec % 64) < 32, -1.0, 1.0)
    mask = np.zeros((128, 512), dtype=np.float32)
    for qc in range(2):
        qloc = qc * 128 + pvec
        mask[:, qc * 256:(qc + 1) * 256] = np.where(
            np.arange(256)[None, :] <= qloc[:, None], 0.0, -1e30)
    identf = np.eye(128, dtype=np.float32).astype(NBF)
    permm = np.zeros((128, 128), dtype=np.float32)
    src = (pvec // 64) * 64 + ((pvec % 64) + 32) % 64
    permm[src, pvec] = 1.0
    permm = permm.astype(NBF)

    in_maps = []
    for core in range(NCORES):
        rs = core * TPC
        posn = (rs % L) + np.arange(TPC, dtype=np.float64)
        theta = posn[None, :] * inv_freq[pvec % 32][:, None]
        cos_t = np.cos(theta).astype(np.float32)
        sin_t = (np.sin(theta) * sign[:, None]).astype(np.float32)
        xcore = np.ascontiguousarray(xfull[rs:rs + TPC].T)
        in_maps.append({
            "xin": xcore, "ffn_w": ffn_stream, "attn_w": attn_stream,
            "cos_t": cos_t, "sin_t": sin_t, "mask_t": mask,
            "idn_t": identf, "perm_t": permm,
        })
    return in_maps


def kernel(L0_attn_x, L0_attn_Wqkv, L0_attn_Wout, L0_ffn_w_gate,
           L0_ffn_w_up, L0_ffn_w_down):
    in_maps = _host_inputs(L0_attn_x, L0_attn_Wqkv, L0_attn_Wout,
                           L0_ffn_w_gate, L0_ffn_w_up, L0_ffn_w_down)
    nc = build_program(sublayers=_SUBLAYERS, debug_taps=_DEBUG_TAPS, debug_attn=_DEBUG_ATTN)
    res = run_bass_kernel_spmd(nc, in_maps, core_ids=list(range(NCORES)))
    if _DEBUG_TAPS:
        global _DEBUG_RES
        _DEBUG_RES = res.results
    shards = [np.asarray(r["xout"]).T for r in res.results]
    return np.concatenate(shards, axis=0).reshape(B, L, D)


# version 12
# speedup vs baseline: 1.4940x; 1.0005x over previous
from contextlib import ExitStack

import numpy as np
import ml_dtypes
import concourse.bass as bass
import concourse.mybir as mybir
from concourse.bass_utils import run_bass_kernel_spmd

F32 = mybir.dt.float32
BF16 = mybir.dt.bfloat16
AF = mybir.ActivationFunctionType
ALU = mybir.AluOpType
AX = mybir.AxisListType
NBF = ml_dtypes.bfloat16

B, L, D = 2, 16384, 1024
HID, NH, HDIM, BS = 3072, 16, 64, 256
NCORES = 8
TPC = (B * L) // NCORES
TILE = 1024
NT = TPC // TILE
DCH = D // 128
HCH = HID // 128
SD = 0.5 / (127.0 * 127.0)
NBH = 4 * NH

_SUBLAYERS = 3

FFN_SLOTS = 72
ATTN_SLOTS = 32
TILE_SLOTS = FFN_SLOTS + ATTN_SLOTS + FFN_SLOTS


class Op:
    __slots__ = ("eng", "fn", "dma", "deps", "needs_inc", "cum", "semkey",
                 "hazard")

    def __init__(self, eng, fn, dma):
        self.eng, self.fn, self.dma = eng, fn, dma
        self.deps = []
        self.needs_inc = False
        self.cum = None
        self.hazard = False
        self.semkey = "D" if (eng == "G" and dma) else eng


class Sched:

    def __init__(self):
        self.ops = {e: [] for e in "SPVAG"}
        self.bufs = {}

    def _buf(self, name):
        b = self.bufs.get(name)
        if b is None:
            b = {"w": None, "r": []}
            self.bufs[name] = b
        return b

    def op(self, eng, fn, reads=(), writes=(), dma=False, semkey=None,
           hazard=False):
        o = Op(eng, fn, dma)
        o.hazard = hazard
        if semkey is not None:
            o.semkey = semkey
        deps = set()
        for n in reads:
            b = self._buf(n)
            if b["w"] is not None:
                deps.add(b["w"])
        for n in writes:
            b = self._buf(n)
            if b["w"] is not None:
                deps.add(b["w"])
            deps.update(b["r"])
        for n in reads:
            self._buf(n)["r"].append(o)
        for n in writes:
            b = self._buf(n)
            b["w"] = o
            b["r"] = []
        for d in deps:
            if d is o:
                continue
            if d.eng == eng and not d.dma and not d.hazard:
                continue
            o.deps.append(d)
            d.needs_inc = True
        self.ops[eng].append(o)
        return o

    SEMKEYS = tuple("SPVAGD") + ("K0", "K1", "K2", "C0", "C1", "X", "X2",
                                 "R0", "R1", "R2", "R3",
                                 "T0", "T1", "T2", "T3")

    def emit(self, block, sems):
        totals = {k: 0 for k in self.SEMKEYS}
        for e, lst in self.ops.items():
            for o in lst:
                if o.needs_inc or o.dma:
                    o.needs_inc = True
                    totals[o.semkey] += 16 if o.dma else 1
                o.cum = totals[o.semkey]

        def body_for(e):
            def body(eng):
                hw = {k: 0 for k in totals}
                for o in self.ops[e]:
                    need = {}
                    for d in o.deps:
                        if d.cum > need.get(d.semkey, 0):
                            need[d.semkey] = d.cum
                    for pe, v in sorted(need.items()):
                        if v > hw[pe]:
                            eng.wait_ge(sems[pe], v)
                            hw[pe] = v
                    inst = o.fn(eng)
                    if o.needs_inc:
                        inst.then_inc(sems[o.semkey], 16 if o.dma else 1)
                if e == "S":
                    for k in ("S", "K0", "K1", "K2", "C0", "C1", "X", "X2",
                              "R0", "R1", "R2", "R3"):
                        if totals[k] > 0:
                            eng.wait_ge(sems[k], totals[k])
                if e == "A":
                    for k in ("T0", "T1", "T2", "T3"):
                        if totals[k] > 0:
                            eng.wait_ge(sems[k], totals[k])
                if e == "G" and totals["D"] > 0:
                    eng.wait_ge(sems["D"], totals["D"])
            return body

        block.sync(body_for("S"))
        block.tensor(body_for("P"))
        block.vector(body_for("V"))
        block.scalar(body_for("A"))
        block.gpsimd(body_for("G"))


_DEBUG_TAPS = False
_DEBUG_ATTN = True


def build_program(sublayers=3, debug_taps=False, debug_attn=True):
    nc = bass.Bass("TRN2", target_bir_lowering=False)

    xin = nc.dram_tensor("xin", [D, TPC], F32, kind="ExternalInput").ap()
    if debug_taps:
        dbg_h = nc.dram_tensor("dbg_h", [HCH, 128, 1024], BF16, kind="ExternalOutput").ap()
        dbg_ao = nc.dram_tensor("dbg_ao", [DCH, 128, 1024], BF16, kind="ExternalOutput").ap()
        dbg_xn = nc.dram_tensor("dbg_xn", [DCH, 128, 1024], BF16, kind="ExternalOutput").ap()
        dbg_x2 = nc.dram_tensor("dbg_x2", [DCH, 128, 1024], F32, kind="ExternalOutput").ap()
        dbg_xn3 = nc.dram_tensor("dbg_xn3", [DCH, 128, 1024], BF16, kind="ExternalOutput").ap()
        dbg_h3 = nc.dram_tensor("dbg_h3", [HCH, 128, 1024], BF16, kind="ExternalOutput").ap()
    xout = nc.dram_tensor("xout", [D, TPC], F32, kind="ExternalOutput").ap()
    ffn_w = nc.dram_tensor("ffn_w", [FFN_SLOTS, 128, 1024], BF16, kind="ExternalInput").ap()
    attn_w = nc.dram_tensor("attn_w", [ATTN_SLOTS, 128, 1024], F32, kind="ExternalInput").ap()
    cos_t = nc.dram_tensor("cos_t", [128, TPC], F32, kind="ExternalInput").ap()
    sin_t = nc.dram_tensor("sin_t", [128, TPC], F32, kind="ExternalInput").ap()
    mask_t = nc.dram_tensor("mask_t", [128, 512], F32, kind="ExternalInput").ap()
    idn_t = nc.dram_tensor("idn_t", [128, 128], BF16, kind="ExternalInput").ap()
    perm_t = nc.dram_tensor("perm_t", [128, 128], BF16, kind="ExternalInput").ap()
    attn_wb = nc.dram_tensor("attn_wb", [ATTN_SLOTS, 128, 1024], BF16, kind="Internal").ap()

    with ExitStack() as ctx:
        def sb(name, shape, dt):
            return ctx.enter_context(nc.sbuf_tensor(name, shape, dt))

        x = [sb(f"x{c}", [128, 1024], F32) for c in range(DCH)]
        xn = [sb(f"xn{c}", [128, 1024], BF16) for c in range(DCH)]
        h = [sb(f"h{c}", [128, 1024], BF16) for c in range(HCH)]
        ao = [sb(f"ao{c}", [128, 1024], BF16) for c in range(DCH)]
        ring = sb("ring", [128, 8 * 1024], BF16)
        cosb = sb("cosb", [128, 1024], F32)
        sinb = sb("sinb", [128, 1024], F32)
        ts_ = [sb(f"t{i}", [128, 1024], F32) for i in range(4)]
        bs = [sb(f"bs{i}", [128, 1024], BF16) for i in range(2)]
        sgb = [sb(f"sgb{i}", [128, 512], BF16) for i in range(2)]
        smv = [sb(f"smv{i}", [128, 512], F32) for i in range(4)]
        ae = [sb(f"ae{i}", [128, 512], BF16) for i in range(4)]
        amn = [sb(f"amn{i}", [128, 512], BF16) for i in range(4)]
        atT = [sb(f"atT{i}", [128, 512], BF16) for i in range(4)]
        negm = [sb(f"negm{i}", [128, 2], F32) for i in range(4)]
        den4 = [sb(f"den4{i}", [128, 4], F32) for i in range(4)]
        rden4 = [sb(f"rden4{i}", [128, 4], F32) for i in range(4)]
        rowsA = sb("rowsA", [1, 1024], F32)
        rowsB = sb("rowsB", [1, 1024], F32)
        rowsC = sb("rowsC", [1, 1024], F32)
        maskb = sb("maskb", [128, 512], F32)
        identb = sb("identb", [128, 128], BF16)
        permb = sb("permb", [128, 128], BF16)
        onesc = sb("onesc", [128, 1], BF16)
        onesr = sb("onesr", [1, 128], F32)
        epsb = sb("epsb", [128, 1], F32)
        pb = [ctx.enter_context(nc.psum_tensor(f"pb{i}", [128, 512], F32)) for i in range(8)]

        sems = {e: ctx.enter_context(nc.semaphore(f"sem{e}")) for e in Sched.SEMKEYS}
        block = ctx.enter_context(nc.Block())

        S = Sched()

        def rname(i):
            return f"ring{i % 8}"

        def rslot(i):
            j = i % 8
            return ring[:, j * 1024:(j + 1) * 1024]

        S.op("S", lambda e: e.dma_start(out=maskb[:, :], in_=mask_t), writes=["maskb"],
             dma=True, semkey="K0")
        S.op("S", lambda e: e.dma_start(out=identb[:, :], in_=idn_t), writes=["identb"],
             dma=True, semkey="K1")
        S.op("S", lambda e: e.dma_start(out=permb[:, :], in_=perm_t), writes=["permb"],
             dma=True, semkey="K2")
        S.op("V", lambda e: e.memset(onesc[:, :], 1.0), writes=["onesc"])
        S.op("V", lambda e: e.memset(onesr[:, :], 1.0), writes=["onesr"])
        S.op("V", lambda e: e.memset(epsb[:, :], 1e-5), writes=["epsb"])
        for i in range(4):
            S.op("V", lambda e, i=i: e.memset(den4[i][:, :], 1.0), writes=[f"den4{i}"])
        for j in range(0, ATTN_SLOTS, 4):
            S.op("G", lambda e, j=j: e.dma_start(out=attn_wb[j:j + 4], in_=attn_w[j:j + 4]),
                 writes=[f"asb{k}" for k in range(j, j + 4)], dma=True)
        S.op("G", lambda e: e.engine_nop(),
             reads=[f"asb{k}" for k in range(ATTN_SLOTS)], writes=["asball"])

        state = {"loaded": 0}

        def src_of(slot):
            s = slot % TILE_SLOTS
            if s < FFN_SLOTS:
                return ffn_w, s, []
            if s < FFN_SLOTS + ATTN_SLOTS:
                j = s - FFN_SLOTS
                return attn_wb, j, ["asball"]
            return ffn_w, s - FFN_SLOTS - ATTN_SLOTS, []

        def need_slot(gi):
            while state["loaded"] <= gi:
                i = state["loaded"]
                arr, j, rd = src_of(i)
                arr2, j2, rd2 = src_of(i + 1)
                assert arr is arr2 and j2 == j + 1 and i % 2 == 0
                pos = i % 8
                out_ap = ring[:, pos * 1024:(pos + 2) * 1024].rearrange(
                    "p (s j) -> p s j", s=2)
                in_ap = arr[j:j + 2].rearrange("s p j -> p s j")
                S.op("S", lambda e, o=out_ap, i_=in_ap: e.dma_start(out=o, in_=i_),
                     reads=rd + rd2, writes=[rname(i), rname(i + 1)], dma=True,
                     semkey=f"R{(i % 8) // 2}")
                state["loaded"] += 2

        def emit_ln(t):
            for c in range(DCH):
                S.op("A", lambda e, c=c: e.copy(out=xn[c][:, :], in_=x[c][:, :]),
                     reads=[f"x{c}"], writes=[f"xn{c}"])
                p = c % 2
                S.op("A", lambda e, c=c, p=p: e.activation(
                    out=bs[p][:, :], in_=xn[c][:, :], func=AF.Square),
                    reads=[f"xn{c}"], writes=[f"bs{p}"])
                for hf in range(2):
                    S.op("P", lambda e, c=c, hf=hf: e.matmul(
                        pb[4 + hf][0:1, 0:512], onesc[:, :],
                        xn[c][:, hf * 512:(hf + 1) * 512],
                        start=(c == 0), stop=(c == DCH - 1)),
                        reads=[f"xn{c}", "onesc"], writes=[f"pb{4 + hf}"])
                    S.op("P", lambda e, c=c, p=p, hf=hf: e.matmul(
                        pb[6 + hf][0:1, 0:512], onesc[:, :],
                        bs[p][:, hf * 512:(hf + 1) * 512],
                        start=(c == 0), stop=(c == DCH - 1)),
                        reads=[f"bs{p}", "onesc"], writes=[f"pb{6 + hf}"])
            for hf in range(2):
                S.op("V", lambda e, hf=hf: e.tensor_scalar(
                    out=rowsA[0:1, hf * 512:(hf + 1) * 512], in0=pb[4 + hf][0:1, 0:512],
                    scalar1=-1.0 / D, scalar2=None, op0=ALU.mult),
                    reads=[f"pb{4 + hf}"], writes=["rowsA"])
            for hf in range(2):
                S.op("P", lambda e, hf=hf: e.matmul(
                    pb[2 + hf][:, 0:512], onesr[0:1, :], rowsA[0:1, hf * 512:(hf + 1) * 512],
                    start=True, stop=True),
                    reads=["rowsA", "onesr"], writes=[f"pb{2 + hf}"])
                S.op("A", lambda e, hf=hf: e.copy(
                    out=ts_[1][:, hf * 512:(hf + 1) * 512], in_=pb[2 + hf][:, 0:512]),
                    reads=[f"pb{2 + hf}"], writes=["t1"])
            for hf in range(2):
                S.op("A", lambda e, hf=hf: e.activation(
                    out=rowsC[0:1, hf * 512:(hf + 1) * 512], in_=pb[4 + hf][0:1, 0:512],
                    func=AF.Square),
                    reads=[f"pb{4 + hf}"], writes=["rowsC"])
                S.op("V", lambda e, hf=hf: e.scalar_tensor_tensor(
                    out=rowsB[0:1, hf * 512:(hf + 1) * 512],
                    in0=rowsC[0:1, hf * 512:(hf + 1) * 512], scalar=-1.0 / D,
                    in1=pb[6 + hf][0:1, 0:512], op0=ALU.mult, op1=ALU.add),
                    reads=["rowsC", f"pb{6 + hf}"], writes=["rowsB"])
            S.op("A", lambda e: e.activation(
                out=rowsB[0:1, 0:1024], in_=rowsB[0:1, 0:1024], func=AF.Ln,
                scale=1.0 / D, bias=epsb[0:1, 0:1]),
                reads=["rowsB", "epsb"], writes=["rowsB"])
            S.op("A", lambda e: e.activation(
                out=rowsC[0:1, 0:1024], in_=rowsB[0:1, 0:1024], func=AF.Exp,
                scale=-0.5),
                reads=["rowsB"], writes=["rowsC"])
            for hf in range(2):
                S.op("P", lambda e, hf=hf: e.matmul(
                    pb[hf][:, 0:512], onesr[0:1, :], rowsC[0:1, hf * 512:(hf + 1) * 512],
                    start=True, stop=True),
                    reads=["rowsC", "onesr"], writes=[f"pb{hf}"])
                S.op("A", lambda e, hf=hf: e.copy(
                    out=ts_[0][:, hf * 512:(hf + 1) * 512], in_=pb[hf][:, 0:512]),
                    reads=[f"pb{hf}"], writes=["t0"])
            for c in range(DCH):
                tp = 2 + (c % 2)
                S.op("G", lambda e, c=c, tp=tp: e.tensor_add(
                    ts_[tp][:, :], x[c][:, :], ts_[1][:, :]),
                    reads=[f"x{c}", "t1"], writes=[f"t{tp}"])
                S.op("V", lambda e, c=c, tp=tp: e.tensor_mul(
                    xn[c][:, :], ts_[tp][:, :], ts_[0][:, :]),
                    reads=[f"t{tp}", "t0"], writes=[f"xn{c}"])

        def emit_ffn(t, base_slot, store_x=False):
            def gu_F0(p):
                hc, th = divmod(p, 2)
                gslot, uslot = base_slot + 2 * hc, base_slot + 2 * hc + 1
                need_slot(uslot)
                gb_, ub_ = (0, 4) if hc % 2 == 0 else (2, 6)
                for kc in range(DCH):
                    S.op("P", lambda e, gslot=gslot, gb_=gb_, th=th, kc=kc: e.matmul(
                        pb[gb_ + th][:, 0:512], rslot(gslot)[:, kc * 128:(kc + 1) * 128],
                        xn[kc][:, th * 512:(th + 1) * 512],
                        start=(kc == 0), stop=(kc == DCH - 1)),
                        reads=[rname(gslot), f"xn{kc}"], writes=[f"pb{gb_ + th}"])
                for kc in range(DCH):
                    S.op("P", lambda e, uslot=uslot, ub_=ub_, th=th, kc=kc: e.matmul(
                        pb[ub_ + th][:, 0:512], rslot(uslot)[:, kc * 128:(kc + 1) * 128],
                        xn[kc][:, th * 512:(th + 1) * 512],
                        start=(kc == 0), stop=(kc == DCH - 1)),
                        reads=[rname(uslot), f"xn{kc}"], writes=[f"pb{ub_ + th}"])

            def gu_F1(p):
                hc, th = divmod(p, 2)
                gb_ = 0 if hc % 2 == 0 else 2
                S.op("A", lambda e, gb_=gb_, th=th: e.activation(
                    out=sgb[th][:, :], in_=pb[gb_ + th][:, 0:512], func=AF.Silu,
                    scale=1.0 / 127.0),
                    reads=[f"pb{gb_ + th}"], writes=[f"sgb{th}"])

            def gu_F2(p):
                hc, th = divmod(p, 2)
                ub_ = 4 if hc % 2 == 0 else 6
                S.op("V", lambda e, hc=hc, ub_=ub_, th=th: e.tensor_mul(
                    h[hc][:, th * 512:(th + 1) * 512], sgb[th][:, :],
                    pb[ub_ + th][:, 0:512]),
                    reads=[f"sgb{th}", f"pb{ub_ + th}"], writes=[f"h{hc}"])

            gu_stages = [gu_F0, gu_F1, gu_F2]
            for tt in range(2 * HCH + 2):
                for k, st in enumerate(gu_stages):
                    p = tt - k
                    if 0 <= p < 2 * HCH:
                        st(p)
            for pas in range(2):
                for hc in range(HCH):
                    slot = base_slot + 48 + pas * 12 + hc // 2
                    need_slot(slot)
                    for oc in range(4):
                        off = (((hc % 2) * 4) + oc) * 128
                        for th in range(2):
                            S.op("P", lambda e, slot=slot, off=off, oc=oc, th=th, hc=hc: e.matmul(
                                pb[oc * 2 + th][:, 0:512],
                                rslot(slot)[:, off:off + 128],
                                h[hc][:, th * 512:(th + 1) * 512],
                                start=(hc == 0), stop=(hc == HCH - 1)),
                                reads=[rname(slot), f"h{hc}"], writes=[f"pb{oc * 2 + th}"])
                for oc in range(4):
                    xc = pas * 4 + oc
                    for th in range(2):
                        S.op("V", lambda e, oc=oc, th=th, xc=xc: e.scalar_tensor_tensor(
                            out=x[xc][:, th * 512:(th + 1) * 512],
                            in0=pb[oc * 2 + th][:, 0:512], scalar=SD,
                            in1=x[xc][:, th * 512:(th + 1) * 512],
                            op0=ALU.mult, op1=ALU.add),
                            reads=[f"pb{oc * 2 + th}", f"x{xc}"], writes=[f"x{xc}"])
                if store_x:
                    for oc in range(4):
                        xc = pas * 4 + oc
                        S.op("G", lambda e, xc=xc, t=t: e.dma_start(
                            out=xout[xc * 128:(xc + 1) * 128, t * 1024:(t + 1) * 1024],
                            in_=x[xc][:, :]),
                            reads=[f"x{xc}"], writes=[f"xo{xc}"], dma=True)
                    S.op("G", lambda e: e.engine_nop(),
                         reads=[f"xo{pas * 4 + oc}" for oc in range(4)],
                         writes=[f"xstb{pas}"])

        def emit_attn(t, base_slot):
            tc0 = t * 1024
            S.op("S", lambda e, t=t: e.dma_start(
                out=cosb[:, :], in_=cos_t[:, t * 1024:(t + 1) * 1024]),
                writes=["cosb"], dma=True, semkey="C0")
            S.op("S", lambda e, t=t: e.dma_start(
                out=sinb[:, :], in_=sin_t[:, t * 1024:(t + 1) * 1024]),
                writes=["sinb"], dma=True, semkey="C1")
            def qk_Q0(j):
                qk, oc = divmod(j, DCH)
                slot = base_slot + qk * 8 + oc
                need_slot(slot)
                qb_ = (0, 1) if j % 2 == 0 else (2, 3)
                for th in range(2):
                    for kc in range(DCH):
                        S.op("P", lambda e, slot=slot, qb_=qb_, th=th, kc=kc: e.matmul(
                            pb[qb_[th]][:, 0:512], rslot(slot)[:, kc * 128:(kc + 1) * 128],
                            xn[kc][:, th * 512:(th + 1) * 512],
                            start=(kc == 0), stop=(kc == DCH - 1)),
                            reads=[rname(slot), f"xn{kc}"], writes=[f"pb{qb_[th]}"])

            def qk_Q1(j):
                qk = j // DCH
                par = j % 2
                qb_ = (0, 1) if par == 0 else (2, 3)
                sw_ = (4, 5) if par == 0 else (6, 7)
                for th in range(2):
                    S.op("A", lambda e, qb_=qb_, th=th, par=par, qk=qk: e.activation(
                        out=bs[par][:, th * 512:(th + 1) * 512], in_=pb[qb_[th]][:, 0:512],
                        func=AF.Copy, scale=(0.125 if qk == 0 else 1.0)),
                        reads=[f"pb{qb_[th]}"], writes=[f"bs{par}"])
                    S.op("P", lambda e, sw_=sw_, th=th, par=par: e.matmul(
                        pb[sw_[th]][:, 0:512], permb[:, :],
                        bs[par][:, th * 512:(th + 1) * 512], start=True, stop=True),
                        reads=["permb", f"bs{par}"], writes=[f"pb{sw_[th]}"])

            def qk_Q2(j):
                par = j % 2
                sw_ = (4, 5) if par == 0 else (6, 7)
                tA, tB = 2 + par, 0 + par
                S.op("V", lambda e, par=par, tA=tA: e.tensor_mul(
                    ts_[tA][:, :], bs[par][:, :], cosb[:, 0:1024]),
                    reads=[f"bs{par}", "cosb"], writes=[f"t{tA}"])
                for th in range(2):
                    S.op("V", lambda e, sw_=sw_, th=th, tB=tB: e.tensor_mul(
                        ts_[tB][:, th * 512:(th + 1) * 512], pb[sw_[th]][:, 0:512],
                        sinb[:, th * 512:(th + 1) * 512]),
                        reads=[f"pb{sw_[th]}", "sinb"], writes=[f"t{tB}"])

            def qk_Q3(j):
                par = j % 2
                tA, tB = 2 + par, 0 + par
                S.op("G", lambda e, hdst=j, tA=tA, tB=tB: e.tensor_add(
                    h[hdst][:, :], ts_[tA][:, :], ts_[tB][:, :]),
                    reads=[f"t{tA}", f"t{tB}"], writes=[f"h{j}"])

            qk_stages = [qk_Q0, qk_Q1, qk_Q2, qk_Q3]
            for tt in range(16 + 3):
                for k, st in enumerate(qk_stages):
                    j = tt - k
                    if 0 <= j < 16:
                        st(j)
            need_slot(base_slot + 23)

            def vp_W0(tc):
                vb_ = (0, 1) if tc % 2 == 0 else (2, 3)
                for vh in range(2):
                    for kc in range(DCH):
                        slot = base_slot + 16 + kc
                        S.op("P", lambda e, slot=slot, vb_=vb_, vh=vh, kc=kc, tc=tc: e.matmul(
                            pb[vb_[vh]][:, 0:512], xn[kc][:, tc * 128:(tc + 1) * 128],
                            rslot(slot)[:, vh * 512:(vh + 1) * 512],
                            start=(kc == 0), stop=(kc == DCH - 1)),
                            reads=[rname(slot), f"xn{kc}"], writes=[f"pb{vb_[vh]}"])

            def vp_W1(tc):
                vb_ = (0, 1) if tc % 2 == 0 else (2, 3)
                for vh in range(2):
                    S.op("A", lambda e, tc=tc, vb_=vb_, vh=vh: e.copy(
                        out=h[16 + tc][:, vh * 512:(vh + 1) * 512], in_=pb[vb_[vh]][:, 0:512]),
                        reads=[f"pb{vb_[vh]}"], writes=[f"h{16 + tc}"])

            for tt in range(8 + 1):
                if tt < 8:
                    vp_W0(tt)
                if tt >= 1:
                    vp_W1(tt - 1)
            def bh_T0(i):
                cp, r = divmod(i, 8)
                hh, b4 = divmod(r, 4)
                qbase, sb_ = hh * 64, (0, 1, 2, 7)[i % 4]
                for qc in range(2):
                    S.op("P", lambda e, sb_=sb_, qc=qc, cp=cp, qbase=qbase, b4=b4: e.matmul(
                        pb[sb_][:, qc * 256:(qc + 1) * 256],
                        h[cp][qbase:qbase + 64, b4 * 256 + qc * 128:b4 * 256 + (qc + 1) * 128],
                        h[8 + cp][qbase:qbase + 64, b4 * 256:(b4 + 1) * 256],
                        start=True, stop=True),
                        reads=[f"h{cp}", f"h{8 + cp}"], writes=[f"pb{sb_}"])

            def bh_T1(i):
                sb_, p2, p4 = (0, 1, 2, 7)[i % 4], i % 4, i % 4
                S.op("V", lambda e, sb_=sb_, p2=p2: e.tensor_add(
                    smv[p2][:, 0:512], pb[sb_][:, 0:512], maskb[:, :]),
                    reads=[f"pb{sb_}", "maskb"], writes=[f"smv{p2}"])
                S.op("V", lambda e, p2=p2, p4=p4: e.reduce_max(
                    negm[p4][:, 0:2],
                    smv[p2][:, 0:512].rearrange("p (c n) -> p c n", c=2),
                    axis=AX.X, negate=True),
                    reads=[f"smv{p2}"], writes=[f"negm{p4}"], hazard=True)

            def bh_T2(i):
                p2, p3, p4 = i % 4, i % 4, i % 4
                for qc in range(2):
                    S.op("A", lambda e, p2=p2, p3=p3, p4=p4, qc=qc: e.activation(
                        out=ae[p3][:, qc * 256:(qc + 1) * 256],
                        in_=smv[p2][:, qc * 256:(qc + 1) * 256], func=AF.Exp,
                        bias=negm[p4][:, qc:qc + 1], scale=1.0,
                        accum_out=den4[p4][:, qc:qc + 1]),
                        reads=[f"smv{p2}", f"negm{p4}"],
                        writes=[f"ae{p3}", f"den4{p4}"], hazard=True)

            def bh_T3(i):
                p3, p4 = i % 4, i % 4
                S.op("V", lambda e, p4=p4: e.reciprocal(
                    rden4[p4][:, 0:4], den4[p4][:, 0:4]),
                    reads=[f"den4{p4}"], writes=[f"rden4{p4}"], hazard=True)
                for qc in range(2):
                    S.op("V", lambda e, p3=p3, p4=p4, qc=qc: e.tensor_scalar(
                        out=amn[p3][:, qc * 256:(qc + 1) * 256],
                        in0=ae[p3][:, qc * 256:(qc + 1) * 256],
                        scalar1=rden4[p4][:, qc:qc + 1], scalar2=None,
                        op0=ALU.mult),
                        reads=[f"ae{p3}", f"rden4{p4}"], writes=[f"amn{p3}"])

            def bh_T4(i):
                tb_, p3 = 3 + (i % 2), i % 4
                tbf = pb[tb_][:, 0:256].bitcast(BF16)
                for kk in range(2):
                    for qc in range(2):
                        S.op("P", lambda e, tbf=tbf, kk=kk, qc=qc, p3=p3: e.transpose(
                            tbf[:, kk * 256 + qc * 128:kk * 256 + (qc + 1) * 128],
                            amn[p3][:, qc * 256 + kk * 128:qc * 256 + (kk + 1) * 128],
                            identb[:, :]),
                            reads=[f"amn{p3}", "identb"], writes=[f"pb{tb_}"])

            def bh_T5(i):
                tb_, p3 = 3 + (i % 2), i % 4
                tbf = pb[tb_][:, 0:256].bitcast(BF16)
                S.op("A", lambda e, tbf=tbf, p3=p3: e.copy(
                    out=atT[p3][:, 0:256], in_=tbf[:, 0:256]),
                    reads=[f"pb{tb_}"], writes=[f"atT{p3}"])
                S.op("V", lambda e, tbf=tbf, p3=p3: e.tensor_copy(
                    out=atT[p3][:, 256:512], in_=tbf[:, 256:512]),
                    reads=[f"pb{tb_}"], writes=[f"atT{p3}"])

            def bh_T6(i):
                cp, r = divmod(i, 8)
                hh, b4 = divmod(r, 4)
                hd, qbase = 2 * cp + hh, hh * 64
                ab_, p3 = 5 + (i % 2), i % 4
                for kk in range(2):
                    S.op("P", lambda e, ab_=ab_, b4=b4, kk=kk, hd=hd, p3=p3, qbase=qbase: e.matmul(
                        pb[ab_][qbase:qbase + 64, 0:256],
                        h[16 + b4 * 2 + kk][:, hd * 64:(hd + 1) * 64],
                        atT[p3][:, kk * 256:(kk + 1) * 256],
                        start=(kk == 0), stop=(kk == 1),
                        tile_position=(0, qbase)),
                        reads=[f"h{16 + b4 * 2 + kk}", f"atT{p3}"], writes=[f"pb{ab_}"])
                S.op("A", lambda e, ab_=ab_, cp=cp, qbase=qbase, b4=b4: e.copy(
                    out=ao[cp][qbase:qbase + 64, b4 * 256:(b4 + 1) * 256],
                    in_=pb[ab_][qbase:qbase + 64, 0:256]),
                    reads=[f"pb{ab_}"], writes=[f"ao{cp}"])

            bh_stages = [bh_T0, bh_T1, bh_T2, bh_T3, bh_T4, bh_T5, bh_T6]
            for tt in range(NBH + len(bh_stages) - 1):
                for k, st in enumerate(bh_stages):
                    i = tt - k
                    if 0 <= i < NBH:
                        st(i)
            for oc in range(DCH):
                slot = base_slot + 24 + oc
                need_slot(slot)
                ob_ = 4 + oc % 4
                for th in range(2):
                    for kc in range(DCH):
                        S.op("P", lambda e, slot=slot, ob_=ob_, th=th, kc=kc: e.matmul(
                            pb[ob_][:, 0:512], rslot(slot)[:, kc * 128:(kc + 1) * 128],
                            ao[kc][:, th * 512:(th + 1) * 512],
                            start=(kc == 0), stop=(kc == DCH - 1)),
                            reads=[rname(slot), f"ao{kc}"], writes=[f"pb{ob_}"])
                    S.op("V", lambda e, oc=oc, th=th, ob_=ob_: e.scalar_tensor_tensor(
                        out=x[oc][:, th * 512:(th + 1) * 512], in0=pb[ob_][:, 0:512],
                        scalar=1.0, in1=x[oc][:, th * 512:(th + 1) * 512],
                        op0=ALU.mult, op1=ALU.add),
                        reads=[f"pb{ob_}", f"x{oc}"], writes=[f"x{oc}"])

        for t in range(NT):
            for c in range(DCH):
                S.op("S", lambda e, c=c, t=t: e.dma_start(
                    out=x[c][:, :], in_=xin[c * 128:(c + 1) * 128, t * 1024:(t + 1) * 1024]),
                    reads=([f"xstb{c // 4}"] if t > 0 else []), writes=[f"x{c}"], dma=True,
                    semkey=("X" if c < 4 else "X2"))
            S.op("V", lambda e: e.engine_nop(), writes=[f"x{c}" for c in range(4)])
            S.op("V", lambda e: e.engine_nop(), writes=[f"x{c}" for c in range(4, DCH)])
            base = t * TILE_SLOTS
            if sublayers >= 1:
                emit_ln(t)
                emit_ffn(t, base)
            if sublayers >= 2:
                emit_ln(t)
                if debug_taps and debug_attn and t == 0:
                    for c in range(DCH):
                        S.op("S", lambda e, c=c: e.dma_start(out=dbg_xn[c], in_=xn[c][:, :]),
                             reads=[f"xn{c}"], writes=[f"dxn{c}"], dma=True)
                emit_attn(t, base + FFN_SLOTS)
                if debug_taps and debug_attn and t == 0:
                    for c in range(HCH):
                        S.op("S", lambda e, c=c: e.dma_start(out=dbg_h[c], in_=h[c][:, :]),
                             reads=[f"h{c}"], writes=[f"dh{c}"], dma=True)
                    for c in range(DCH):
                        S.op("S", lambda e, c=c: e.dma_start(out=dbg_ao[c], in_=ao[c][:, :]),
                             reads=[f"ao{c}"], writes=[f"dao{c}"], dma=True)
            if sublayers >= 3:
                if debug_taps and t == 0:
                    for c in range(DCH):
                        S.op("S", lambda e, c=c: e.dma_start(out=dbg_x2[c], in_=x[c][:, :]),
                             reads=[f"x{c}"], writes=[f"dx2{c}"], dma=True)
                emit_ln(t)
                if debug_taps and t == 0:
                    for c in range(DCH):
                        S.op("S", lambda e, c=c: e.dma_start(out=dbg_xn3[c], in_=xn[c][:, :]),
                             reads=[f"xn{c}"], writes=[f"dxn3{c}"], dma=True)
                emit_ffn(t, base + FFN_SLOTS + ATTN_SLOTS, store_x=True)
                if debug_taps and t == 0:
                    for c in range(HCH):
                        S.op("S", lambda e, c=c: e.dma_start(out=dbg_h3[c], in_=h[c][:, :]),
                             reads=[f"h{c}"], writes=[f"dh3{c}"], dma=True)
            else:
                state["loaded"] = (t + 1) * TILE_SLOTS
            if sublayers < 3:
                for c in range(DCH):
                    S.op("G", lambda e, c=c, t=t: e.dma_start(
                        out=xout[c * 128:(c + 1) * 128, t * 1024:(t + 1) * 1024], in_=x[c][:, :]),
                        reads=[f"x{c}"], writes=[f"xo{c}"], dma=True)
                for pas in range(2):
                    S.op("G", lambda e: e.engine_nop(),
                         reads=[f"xo{pas * 4 + oc}" for oc in range(4)],
                         writes=[f"xstb{pas}"])

        S.emit(block, sems)
    return nc


def _host_inputs(L0_attn_x, L0_attn_Wqkv, L0_attn_Wout, L0_ffn_w_gate,
                 L0_ffn_w_up, L0_ffn_w_down):
    xfull = np.ascontiguousarray(np.asarray(L0_attn_x, dtype=np.float32)).reshape(B * L, D)
    wqkv = np.asarray(L0_attn_Wqkv, dtype=np.float32)
    wout = np.asarray(L0_attn_Wout, dtype=np.float32)
    wg = np.asarray(L0_ffn_w_gate)
    wu = np.asarray(L0_ffn_w_up)
    wd = np.asarray(L0_ffn_w_down)

    def lhsT_slots(w):
        return np.ascontiguousarray(
            w.reshape(8, 128, 8, 128).transpose(0, 3, 2, 1).reshape(8, 128, 1024))

    wg_s = np.ascontiguousarray(
        wg.reshape(HCH, 128, 8, 128).transpose(0, 3, 2, 1).reshape(HCH, 128, 1024))
    wu_s = np.ascontiguousarray(
        wu.reshape(HCH, 128, 8, 128).transpose(0, 3, 2, 1).reshape(HCH, 128, 1024))
    gu = np.stack([wg_s, wu_s], axis=1).reshape(2 * HCH, 128, 1024)
    wd6 = wd.reshape(2, 4, 128, 12, 2, 128).transpose(0, 3, 5, 4, 1, 2)
    wd_s = np.ascontiguousarray(wd6.reshape(24, 128, 1024))
    ffn_stream = np.concatenate([gu, wd_s], axis=0).astype(NBF)
    assert ffn_stream.shape == (FFN_SLOTS, 128, 1024)

    pj = np.concatenate([np.arange(0, 64, 2), np.arange(1, 64, 2)])
    perm_rows = (np.arange(1024) // 64) * 64 + pj[np.arange(1024) % 64]
    wq_p = wqkv[0:1024][perm_rows]
    wk_p = wqkv[1024:2048][perm_rows]
    wv = wqkv[2048:3072]
    vs = np.ascontiguousarray(wv.reshape(1024, 8, 128).transpose(1, 2, 0))
    attn_stream = np.concatenate(
        [lhsT_slots(wq_p), lhsT_slots(wk_p), vs, lhsT_slots(wout)],
        axis=0).astype(np.float32)
    assert attn_stream.shape == (ATTN_SLOTS, 128, 1024)

    pvec = np.arange(128)
    inv_freq = 1.0 / 10000.0 ** (np.arange(0, HDIM, 2, dtype=np.float64) / HDIM)
    sign = np.where((pvec % 64) < 32, -1.0, 1.0)
    mask = np.zeros((128, 512), dtype=np.float32)
    for qc in range(2):
        qloc = qc * 128 + pvec
        mask[:, qc * 256:(qc + 1) * 256] = np.where(
            np.arange(256)[None, :] <= qloc[:, None], 0.0, -1e30)
    identf = np.eye(128, dtype=np.float32).astype(NBF)
    permm = np.zeros((128, 128), dtype=np.float32)
    src = (pvec // 64) * 64 + ((pvec % 64) + 32) % 64
    permm[src, pvec] = 1.0
    permm = permm.astype(NBF)

    in_maps = []
    for core in range(NCORES):
        rs = core * TPC
        posn = (rs % L) + np.arange(TPC, dtype=np.float64)
        theta = posn[None, :] * inv_freq[pvec % 32][:, None]
        cos_t = np.cos(theta).astype(np.float32)
        sin_t = (np.sin(theta) * sign[:, None]).astype(np.float32)
        xcore = np.ascontiguousarray(xfull[rs:rs + TPC].T)
        in_maps.append({
            "xin": xcore, "ffn_w": ffn_stream, "attn_w": attn_stream,
            "cos_t": cos_t, "sin_t": sin_t, "mask_t": mask,
            "idn_t": identf, "perm_t": permm,
        })
    return in_maps


def kernel(L0_attn_x, L0_attn_Wqkv, L0_attn_Wout, L0_ffn_w_gate,
           L0_ffn_w_up, L0_ffn_w_down):
    in_maps = _host_inputs(L0_attn_x, L0_attn_Wqkv, L0_attn_Wout,
                           L0_ffn_w_gate, L0_ffn_w_up, L0_ffn_w_down)
    nc = build_program(sublayers=_SUBLAYERS, debug_taps=_DEBUG_TAPS, debug_attn=_DEBUG_ATTN)
    res = run_bass_kernel_spmd(nc, in_maps, core_ids=list(range(NCORES)))
    if _DEBUG_TAPS:
        global _DEBUG_RES
        _DEBUG_RES = res.results
    shards = [np.asarray(r["xout"]).T for r in res.results]
    return np.concatenate(shards, axis=0).reshape(B, L, D)
